# version 03 (final)
# baseline (speedup 1.0000x reference)
import math
import numpy as np

import concourse.bass as bass
import concourse.mybir as mybir
from concourse.bass_utils import run_bass_kernel_spmd

F32 = mybir.dt.float32
BF16 = mybir.dt.bfloat16
AF = mybir.ActivationFunctionType
ALU = mybir.AluOpType
AX = mybir.AxisListType

DM = 1024
INNER = 512
HEADS = 8
DH = 64
WIN = 512
HID = 2730
HIDP = 2816
W1P = 2 * HIDP
NPAIR = HIDP // 128
DI = 2048
DS = 16
DC = 4
QK_SCALE = 8.0

ENGS = ("tensor", "vector", "scalar", "gpsimd", "sync")

BULK_SEMS = {"ldc", "ldcf", "wq", "wq2", "wo", "w1", "w2", "wmr", "wmr2",
             "wmi", "wmc", "wmx", "wmo", "vtl"}


class Cfg:
    def __init__(self, T=4096, NB=2, paranoid=False):
        assert T % 512 == 0
        self.T = T
        self.NB = NB
        self.CH = 512
        self.NCH = T // self.CH
        self.NT = T // 128
        self.paranoid = paranoid



class _Op:
    __slots__ = ("idx", "eng", "fn", "reads", "writes", "dma", "tag",
                 "waits", "pre_drain", "milestone", "bar")

    def __init__(self, idx, eng, fn, reads, writes, dma, tag):
        self.idx = idx
        self.eng = eng
        self.fn = fn
        self.reads = reads
        self.writes = writes
        self.dma = dma
        self.tag = tag
        self.waits = []
        self.pre_drain = False
        self.milestone = False
        self.bar = False


class Prog:
    def __init__(self, nc, paranoid=False):
        self.nc = nc
        self.paranoid = paranoid
        self.ops = []
        self.res_w = {}
        self.res_r = {}
        self.dma_tot = {}
        self.know = {e: {} for e in ENGS}
        self.snap = {}
        self.drained_upto = {e: -1 for e in ENGS}

    def op(self, eng, fn, reads=(), writes=(), tag=""):
        return self._rec(eng, fn, reads, writes, None, tag)

    def dma(self, eng, out, in_, semkey, reads=(), writes=(), tag="", **kw):
        if out.dtype != in_.dtype:
            assert eng == "gpsimd", f"cast dma must be gpsimd: {tag}"

        def fn(e, out=out, in_=in_, kw=kw):
            return e.dma_start(out=out, in_=in_, **kw)
        return self._rec(eng, fn, reads, writes, semkey, tag)

    def _rec(self, eng, fn, reads, writes, semkey, tag):
        idx = len(self.ops)
        o = _Op(idx, eng, fn, tuple(reads), tuple(writes), semkey, tag)
        self.ops.append(o)
        know = self.know[eng]

        def need(prod):
            if prod is None:
                return
            if prod[0] == "E":
                _, peng, pidx = prod
                if peng == eng:
                    if self.paranoid and semkey is None and self.drained_upto[eng] < pidx:
                        o.pre_drain = True
                        self.drained_upto[eng] = idx - 1
                    return
                if know.get(("E", peng), -1) >= pidx:
                    return
                o.waits.append(prod)
                know[("E", peng)] = pidx
                s = self.snap.get((peng, pidx))
                if s:
                    for k, v in s.items():
                        if know.get(k, -1) < v:
                            know[k] = v
            else:
                _, sk, cnt = prod
                if sk in BULK_SEMS:
                    cnt = self.dma_tot[sk]
                if know.get(("D", sk), -1) >= cnt:
                    return
                o.waits.append(("D", sk, cnt))
                know[("D", sk)] = cnt

        for r in reads:
            need(self.res_w.get(r))
        for w in writes:
            need(self.res_w.get(w))
            for rd in self.res_r.get(w, ()):
                need(rd)

        if semkey is not None:
            self.dma_tot[semkey] = self.dma_tot.get(semkey, 0) + 1
            me = ("D", semkey, self.dma_tot[semkey])
        else:
            me = ("E", eng, idx)
            self.snap[(eng, idx)] = dict(know)
        for w in writes:
            self.res_w[w] = me
            self.res_r[w] = []
        for r in reads:
            self.res_r.setdefault(r, []).append(me)
        return o

    def barrier(self):
        idx = len(self.ops)
        for e in ENGS:
            o = _Op(idx, e, None, (), (), None, "barrier")
            o.bar = True
            self.ops.append(o)
            idx += 1
        last = {}
        for o in self.ops:
            if not o.bar and o.dma is None:
                last[o.eng] = o.idx
        for e in ENGS:
            k = self.know[e]
            for e2, li in last.items():
                if e2 != e and k.get(("E", e2), -1) < li:
                    k[("E", e2)] = li
            for sk, c in self.dma_tot.items():
                if k.get(("D", sk), -1) < c:
                    k[("D", sk)] = c

    def emit(self):
        nc = self.nc
        for o in self.ops:
            for w in o.waits:
                if w[0] == "E":
                    self.ops[w[2]].milestone = True
        i = 0
        while i < len(self.ops):
            if self.ops[i].bar:
                last = {}
                dtot = {}
                for o in self.ops[:i]:
                    if o.bar:
                        continue
                    if o.dma is not None:
                        dtot[o.dma] = dtot.get(o.dma, 0) + 1
                    else:
                        last[o.eng] = o.idx
                for j, e in enumerate(ENGS):
                    b = self.ops[i + j]
                    b.waits = [("E", e2, li) for e2, li in last.items() if e2 != e]
                    b.waits += [("D", sk, c) for sk, c in dtot.items()]
                for e2, li in last.items():
                    self.ops[li].milestone = True
                i += len(ENGS)
            else:
                i += 1

        esem = {e: nc.alloc_semaphore(f"es_{e}") for e in ENGS}
        dsem = {sk: nc.alloc_semaphore(f"ds_{sk}") for sk in self.dma_tot}
        wv = {}
        run = {e: 0 for e in ENGS}
        for o in self.ops:
            if o.bar or o.dma is not None:
                continue
            if o.milestone:
                run[o.eng] += 1
            wv[o.idx] = run[o.eng]

        byeng = {e: [o for o in self.ops if o.eng == e] for e in ENGS}
        stats = {e: len(byeng[e]) for e in ENGS}

        with nc.Block() as block:
            def mk(e_name):
                def body(eng):
                    for o in byeng[e_name]:
                        seen = {}
                        for w in o.waits:
                            if w[0] == "E":
                                k = ("E", w[1])
                                v = wv[w[2]]
                            else:
                                k = ("D", w[1])
                                v = 16 * w[2]
                            if seen.get(k, -1) < v:
                                seen[k] = v
                        for (kind, name), v in seen.items():
                            eng.wait_ge(esem[name] if kind == "E" else dsem[name], v)
                        if o.bar:
                            continue
                        if o.pre_drain:
                            eng.drain()
                        inst = o.fn(eng)
                        if o.dma is not None:
                            inst.then_inc(dsem[o.dma], 16)
                        elif o.milestone:
                            if inst is None:
                                raise RuntimeError(f"milestone op returned None: {o.tag}")
                            inst.then_inc(esem[o.eng], 1)
                return body

            block.sync(mk("sync"))
            block.tensor(mk("tensor"))
            block.vector(mk("vector"))
            block.scalar(mk("scalar"))
            block.gpsimd(mk("gpsimd"))
        return stats



def host_consts(cfg):
    T = cfg.T
    c = {}
    c["identc"] = np.eye(128, dtype=np.float32)
    c["tric"] = np.triu(np.ones((128, 128), dtype=np.float32))
    c["ones128"] = np.ones((128, 128), dtype=np.float32)
    en = np.zeros((16, 16 * 128), dtype=np.float32)
    for n in range(16):
        en[n, n * 128:(n + 1) * 128] = 1.0
    c["ensel"] = en
    c["onesrow"] = np.ones((1, 512), dtype=np.float32)
    c["negonesrow"] = -np.ones((1, 128), dtype=np.float32)
    c["onescol"] = np.ones((128, 1), dtype=np.float32)
    i = np.arange(128)
    c["maskhuge"] = np.where(i[:, None] > i[None, :], 1e30, 0.0).astype(np.float32)
    r = np.arange(128)[:, None]
    q = np.arange(256)[None, :]
    for j in (0, 1, 4, 5):
        d = j * 128 + r - q
        c[f"m01_{j}"] = ((d >= 0) & (d <= WIN)).astype(np.float32)
    c["cpool"] = np.tile(np.array([[1e-5, 1e-20, 1.0, 0.0]], dtype=np.float32), (128, 1))

    half = np.arange(0, DH, 2, dtype=np.float64)
    inv_freq = 1.0 / 10000.0 ** (half / DH)
    t = np.arange(T, dtype=np.float64)
    freqs = t[:, None] * inv_freq[None, :]
    base = (half + 0.4 * DH) / (1.4 * DH)
    power = (t - T // 2) / float(WIN // 2)
    scale = base[None, :] ** power[:, None]
    cos = np.cos(freqs)
    sin = np.sin(freqs)

    def tokmaj(a):
        nt = T // 128
        return np.ascontiguousarray(
            a.reshape(nt, 128, 32).transpose(1, 0, 2).reshape(128, nt * 32)
        ).astype(np.float32)

    c["csq"] = tokmaj(cos * scale)
    c["snq"] = tokmaj(sin * scale)
    c["csk"] = tokmaj(cos / scale)
    c["snk"] = tokmaj(sin / scale)
    return c


def host_weights(inp):
    w = {}
    for l in range(2):
        w[f"wqkv{l}"] = np.ascontiguousarray(inp["L0_ffn_wqkv"][l])
        w[f"wo{l}"] = np.ascontiguousarray(inp["L0_ffn_wo"][l])
        w1 = inp["L0_ffn_w1"][l]
        a = np.zeros((DM, HIDP), dtype=np.float32)
        g = np.zeros((DM, HIDP), dtype=np.float32)
        a[:, :HID] = w1[:, :HID]
        g[:, :HID] = w1[:, HID:]
        w1p = np.zeros((DM, W1P), dtype=np.float32)
        for tt in range(NPAIR):
            w1p[:, (2 * tt) * 128:(2 * tt + 1) * 128] = a[:, tt * 128:(tt + 1) * 128]
            w1p[:, (2 * tt + 1) * 128:(2 * tt + 2) * 128] = g[:, tt * 128:(tt + 1) * 128]
        w[f"w1{l}"] = w1p
        b1 = inp["L0_ffn_b1"][l]
        b1c = np.zeros((128, 2 * NPAIR), dtype=np.float32)
        for tt in range(NPAIR):
            sl = slice(tt * 128, (tt + 1) * 128)
            apart = np.zeros(128, dtype=np.float32)
            gpart = np.zeros(128, dtype=np.float32)
            asrc = b1[:HID][sl]
            gsrc = b1[HID:][sl]
            apart[:len(asrc)] = asrc
            gpart[:len(gsrc)] = gsrc
            b1c[:, 2 * tt] = apart
            b1c[:, 2 * tt + 1] = gpart
        w[f"b1c{l}"] = b1c
        w2 = inp["L0_ffn_w2"][l]
        w2p = np.zeros((HIDP, DM), dtype=np.float32)
        w2p[:HID, :] = w2
        w[f"w2{l}"] = w2p
        w[f"b2row{l}"] = np.ascontiguousarray(inp["L0_ffn_b2"][l].reshape(1, DM))
        w[f"lgc{l}"] = np.ascontiguousarray(inp["L0_ffn_ln_g"][l].reshape(8, 128).T)
        w[f"lbc{l}"] = np.ascontiguousarray(inp["L0_ffn_ln_b"][l].reshape(8, 128).T)
        w[f"qsrow{l}"] = np.ascontiguousarray(
            np.tile(inp["L0_ffn_q_scale"][l].reshape(1, DH), (1, HEADS)))
        w[f"ksrow{l}"] = np.ascontiguousarray(
            np.tile(inp["L0_ffn_k_scale"][l].reshape(1, DH), (1, HEADS)))
    w["win"] = np.ascontiguousarray(inp["L0_attn_W_in"])
    w["wout"] = np.ascontiguousarray(inp["L0_attn_W_out"])
    wx = inp["L0_attn_W_x"]
    wx96 = np.zeros((DI, 96), dtype=np.float32)
    wx96[:, 0] = wx[:, 0]
    wx96[:, 32:48] = wx[:, 1:17]
    wx96[:, 64:80] = wx[:, 17:33]
    w["wx96"] = wx96
    cw = inp["L0_attn_conv_w"][:, 0, :]
    diag = np.zeros((DC, 16, 128, 128), dtype=np.float32)
    for k in range(DC):
        for ft in range(16):
            np.fill_diagonal(diag[k, ft], cw[ft * 128:(ft + 1) * 128, k])
    w["convdiag"] = np.ascontiguousarray(diag.reshape(DC * 16 * 128, 128))
    w["convbcol"] = np.ascontiguousarray(inp["L0_attn_conv_b"].reshape(16, 128).T)
    w["dcol"] = np.ascontiguousarray(inp["L0_attn_D"].reshape(16, 128).T)
    w["alogcol"] = np.ascontiguousarray(inp["L0_attn_A_log"].reshape(16, 1))
    return w


WSHAPES = {
    "win": [DM, 2 * DI], "wout": [DI, DM], "wx96": [DI, 96],
    "convdiag": [DC * 16 * 128, 128], "convbcol": [128, 16], "dcol": [128, 16],
    "alogcol": [16, 1],
}
for _l in range(2):
    WSHAPES.update({
        f"wqkv{_l}": [DM, 3 * INNER], f"wo{_l}": [INNER, DM],
        f"w1{_l}": [DM, W1P], f"b1c{_l}": [128, 2 * NPAIR], f"w2{_l}": [HIDP, DM],
        f"b2row{_l}": [1, DM], f"lgc{_l}": [128, 8], f"lbc{_l}": [128, 8],
        f"qsrow{_l}": [1, INNER], f"ksrow{_l}": [1, INNER],
    })


def cshapes(cfg):
    d = {
        "identc": [128, 128], "tric": [128, 128], "ones128": [128, 128], "ensel": [16, 16 * 128],
        "onesrow": [1, 512], "negonesrow": [1, 128], "onescol": [128, 1],
        "maskhuge": [128, 128], "cpool": [128, 4],
        "csq": [128, cfg.NT * 32], "snq": [128, cfg.NT * 32],
        "csk": [128, cfg.NT * 32], "snk": [128, cfg.NT * 32],
    }
    for j in (0, 1, 4, 5):
        d[f"m01_{j}"] = [128, 256]
    return d



from contextlib import ExitStack


class KB:
    def __init__(self, cfg):
        self.cfg = cfg
        nc = bass.Bass("TRN2", target_bir_lowering=False,
                       detect_race_conditions=cfg.paranoid)
        self.nc = nc
        self.prog = Prog(nc, paranoid=cfg.paranoid)
        T = cfg.T

        self.io = {}
        self.io["x0"] = nc.dram_tensor("x0", [cfg.NB, DM, T], F32, kind="ExternalInput").ap()
        self.io["out"] = nc.dram_tensor("out", [cfg.NB, DM, T], F32, kind="ExternalOutput").ap()
        for n, sh in {**WSHAPES, **cshapes(cfg)}.items():
            self.io[n] = nc.dram_tensor(n, sh, F32, kind="ExternalInput").ap()
        import os as _osd
        _K = "ExternalOutput" if _osd.environ.get("DBG_SCRATCH") else "Internal"
        if _osd.environ.get("DBG_SCRATCH"):
            self.io["dbg1"] = nc.dram_tensor("dbg1", [128, 16], F32, kind="ExternalOutput").ap()
            self.io["dbg2"] = nc.dram_tensor("dbg2", [128, 512], F32, kind="ExternalOutput").ap()
            self.io["dbg3"] = nc.dram_tensor("dbg3", [128, 512], F32, kind="ExternalOutput").ap()
        self.io["x1"] = nc.dram_tensor("x1s", [DM, T], F32, kind=_K).ap()
        self.io["x2"] = nc.dram_tensor("x2s", [DM, T], F32, kind=_K).ap()
        for v in ("vA", "vB", "vC"):
            self.io[v] = nc.dram_tensor(v + "s", [DM, 1 + T], F32, kind=_K).ap()
        self.io["qfm"] = nc.dram_tensor("qfms", [INNER, T], BF16, kind=_K).ap()
        self.io["kfm"] = nc.dram_tensor("kfms", [INNER, T], BF16, kind=_K).ap()
        self.io["vtk"] = nc.dram_tensor("vtks", [128, cfg.NT * 8 * 65], BF16, kind="Internal").ap()
        self.io["glu"] = nc.dram_tensor("glus", [128, cfg.NCH * NPAIR * 512], BF16, kind="Internal").ap()
        self.io["xs"] = nc.dram_tensor("xss", [128, cfg.NCH * 16 * 512], BF16, kind="Internal").ap()
        self.io["sz"] = nc.dram_tensor("szs", [128, cfg.NCH * 16 * 512], BF16, kind="Internal").ap()
        self.io["xdb"] = nc.dram_tensor("xdbs", [96, T], F32, kind="Internal").ap()
        self.io["cbr"] = nc.dram_tensor("cbrs", [16, T], F32, kind="Internal").ap()
        self.io["cum"] = nc.dram_tensor("cums", [1, T], F32, kind="Internal").ap()

        self.PS1 = nc.alloc_psum_tensor("PSA", [128, 2048], F32)
        self.PS2 = nc.alloc_psum_tensor("PSB", [128, 2048], F32)

        A = nc.alloc_sbuf_tensor
        self.ident = A("c_ident", [128, 128], BF16)
        self.identf = A("c_identf", [128, 128], F32)
        self.tri = A("c_tri", [128, 128], BF16)
        self.ones128 = A("c_ones128", [128, 128], BF16)
        self.ensel = A("c_ensel", [16, 16 * 128], BF16)
        self.onesrow = A("c_onesrow", [1, 512], BF16)
        self.onesrowf = A("c_onesrowf", [1, 512], F32)
        self.negonesrow = A("c_negonesrow", [1, 128], BF16)
        self.onescol = A("c_onescol", [128, 1], BF16)
        self.maskhuge = A("c_maskhuge", [128, 128], F32)
        self.cpool = A("c_cpool", [128, 4], F32)
        nc.const_aps.register(F32, 1e-5, self.cpool[:, 0:1])
        nc.const_aps.register(F32, 1e-20, self.cpool[:, 1:2])
        self.m01 = {j: A(f"c_m01_{j}", [128, 256], BF16) for j in (0, 1, 4, 5)}
        self.convb = A("c_convb", [128, 16], F32)
        self.dcol = A("c_dcol", [128, 16], F32)
        self.acols = A("c_acols", [16, 2], F32)
        self.abc = A("c_abc", [128, 16], F32)
        self.narow = A("c_narow", [1, 16], F32)
        self.rowsc = A("c_rows", [1, 2048], F32)
        self.xst = [A(f"p_xst{i}", [128, 8 * 513], F32) for i in range(2)]
        self.vout = [A(f"p_vout{i}", [128, 512], F32) for i in range(2)]
        self.lnw = A("p_lnw", [128, 512], F32)
        self.zc = A("p_zc", [128, 8], F32)

        self._scopes = []

        p = self.prog
        io = self.io
        castl = [
            (self.ident, "identc"), (self.tri, "tric"), (self.ensel, "ensel"),
            (self.ones128, "ones128"), (self.onesrow, "onesrow"), (self.negonesrow, "negonesrow"),
            (self.onescol, "onescol"),
        ] + [(self.m01[j], f"m01_{j}") for j in (0, 1, 4, 5)]
        for t_, n_ in castl:
            p.dma("gpsimd", t_[:], io[n_], "ldc", writes=[("c", n_)], tag=f"ldc_{n_}")
        plain = [
            (self.identf, "identc"), (self.onesrowf, "onesrow"),
            (self.maskhuge, "maskhuge"), (self.cpool, "cpool"),
            (self.convb, "convbcol"), (self.dcol, "dcol"),
        ]
        for t_, n_ in plain:
            p.dma("sync", t_[:], io[n_], "ldcf", writes=[("c", "f" + n_)], tag=f"ldcf_{n_}")
        p.dma("sync", self.acols[0:16, 1:2], io["alogcol"], "ldcf",
              writes=[("al",)], tag="ld_alog")
        p.op("scalar", lambda e: e.activation(self.acols[0:16, 1:2], self.acols[0:16, 1:2], AF.Exp),
             reads=[("al",)], writes=[("negA",)], tag="negA")
        p.op("vector", lambda e: e.tensor_scalar(self.acols[0:16, 0:1], self.acols[0:16, 1:2], -1.0, None, ALU.mult),
             reads=[("negA",)], writes=[("Acol",)], tag="Acol")
        p.op("tensor", lambda e: e.transpose(self.PS1[0:1, 0:16], self.acols[0:16, 0:1], self.identf[0:16, 0:16]),
             reads=[("Acol",), ("c", "fidentc")], writes=[("ps", 0)], tag="arowT")
        p.op("scalar", lambda e: e.copy(self.rowsc[0:1, 0:16], self.PS1[0:1, 0:16]),
             reads=[("ps", 0)], writes=[("Arowtmp",)], tag="arow")
        p.op("tensor", lambda e: e.matmul(self.PS1[:, 512:528], self.onesrowf[0:1, 0:128],
                                          self.rowsc[0:1, 0:16], start=True, stop=True),
             reads=[("Arowtmp",), ("c", "fonesrow")], writes=[("ps", 1)], tag="abcmm")
        p.op("scalar", lambda e: e.copy(self.abc[:], self.PS1[:, 512:528]),
             reads=[("ps", 1)], writes=[("abc",)], tag="abc")
        self.nabc = A("c_nabc", [128, 16], F32)
        p.op("vector", lambda e: e.tensor_scalar(self.nabc[:], self.abc[:], -1.0, None, ALU.mult),
             reads=[("abc",)], writes=[("nabc",)], tag="nabc")
        p.op("tensor", lambda e: e.transpose(self.PS1[0:1, 0:16], self.acols[0:16, 1:2], self.identf[0:16, 0:16]),
             reads=[("negA",), ("c", "fidentc"), ("Arowtmp",)], writes=[("ps", 0)], tag="negArowT")
        p.op("scalar", lambda e: e.copy(self.narow[0:1, :], self.PS1[0:1, 0:16]),
             reads=[("ps", 0)], writes=[("narow",)], tag="negArow")

    def open_scope(self):
        self._scopes.append(ExitStack())
        return self._scopes[-1]

    def salloc(self, name, shape, dtype):
        st = self._scopes[-1]
        guard = self.nc.sbuf_tensor(f"{name}_{len(self.prog.ops)}", shape, dtype)
        return st.enter_context(guard)

    def close_scope(self):
        self._scopes.pop().close()

    def ps(self, bank, off=0, n=512):
        base = self.PS1 if bank < 4 else self.PS2
        b = bank % 4
        return base[:, b * 512 + off: b * 512 + off + n]

    def psb16(self, bank, off=0, n=512):
        base = self.PS1 if bank < 4 else self.PS2
        b = bank % 4
        return base.bitcast(BF16)[:, b * 1024 + off: b * 1024 + off + n]

    def pk(self, bank):
        return ("ps", bank)

    def ln_chunk(self, L, src_views, okey, hn_dst, hn_key, affine, v0_dst=None,
                 v0_key=None, v0_post=None, banks=(6, 7, 2, 3), tag=""):
        p, ps, pk = self.prog, self.ps, self.pk
        bS1, bS2, bP, bQ = banks
        xb = L.lnxb
        xsq = L.lnxsq
        for k in range(8):
            p.op("vector", lambda e, k=k: e.tensor_copy(xb[:, k * 512:(k + 1) * 512], src_views[k]),
                 reads=okey, writes=[("lnxb", k)], tag=f"{tag}lncp{k}")
            p.op("scalar", lambda e, k=k: e.activation(xsq[:, k * 512:(k + 1) * 512], src_views[k], AF.Square),
                 reads=okey, writes=[("lnxsq", k)], tag=f"{tag}lnsq{k}")

        def mm_s1(e):
            r = None
            for k in range(8):
                r = e.matmul(ps(bS1, 0, 512)[0:1, :], self.onescol[:], xb[:, k * 512:(k + 1) * 512],
                             start=(k == 0), stop=(k == 7))
            return r

        def mm_s2(e):
            r = None
            for k in range(8):
                r = e.matmul(ps(bS2, 0, 512)[0:1, :], self.onescol[:], xsq[:, k * 512:(k + 1) * 512],
                             start=(k == 0), stop=(k == 7))
            return r
        p.op("tensor", mm_s1, reads=[("lnxb", k) for k in range(8)] + [("c", "onescol")],
             writes=[pk(bS1)], tag=f"{tag}lnS1")
        p.op("tensor", mm_s2, reads=[("lnxsq", k) for k in range(8)] + [("c", "onescol")],
             writes=[pk(bS2)], tag=f"{tag}lnS2")
        rows = self.rowsc
        mu = rows[0:1, 0:512]
        var = rows[0:1, 512:1024]
        Pr = rows[0:1, 1024:1536]
        Qr = rows[0:1, 1536:2048]
        p.op("vector", lambda e: e.tensor_scalar(mu, ps(bS1, 0, 512)[0:1, :], 1.0 / DM, None, ALU.mult),
             reads=[pk(bS1)], writes=[("r", "mu")], tag=f"{tag}mu")
        p.op("vector", lambda e: e.tensor_scalar(var, ps(bS2, 0, 512)[0:1, :], 1.0 / DM, None, ALU.mult),
             reads=[pk(bS2)], writes=[("r", "var")], tag=f"{tag}var")
        p.op("scalar", lambda e: e.activation(Pr, mu, AF.Square),
             reads=[("r", "mu")], writes=[("r", "mu2")], tag=f"{tag}mu2")
        p.op("vector", lambda e: e.tensor_tensor(var, var, Pr, ALU.subtract),
             reads=[("r", "var"), ("r", "mu2")], writes=[("r", "var")], tag=f"{tag}var2")
        p.op("scalar", lambda e: e.activation(Pr, var, AF.Ln, bias=1e-5),
             reads=[("r", "var")], writes=[("r", "lnv")], tag=f"{tag}lnv")
        p.op("scalar", lambda e: e.activation(Pr, Pr, AF.Exp, scale=-0.5),
             reads=[("r", "lnv")], writes=[("r", "P")], tag=f"{tag}rstd")
        p.op("vector", lambda e: e.scalar_tensor_tensor(Qr, mu, -1.0, Pr, ALU.mult, ALU.mult),
             reads=[("r", "mu"), ("r", "P")], writes=[("r", "Q")], tag=f"{tag}Q")
        p.op("tensor", lambda e: e.matmul(ps(bP, 0, 512), self.onesrowf[0:1, 0:128], Pr, start=True, stop=True),
             reads=[("r", "P"), ("c", "fonesrow")], writes=[pk(bP)], tag=f"{tag}Pbc")
        p.op("tensor", lambda e: e.matmul(ps(bQ, 0, 512), self.onesrowf[0:1, 0:128], Qr, start=True, stop=True),
             reads=[("r", "Q"), ("c", "fonesrow")], writes=[pk(bQ)], tag=f"{tag}Qbc")
        w = self.lnw[:]
        for k in range(8):
            p.op("vector", lambda e, k=k: e.tensor_tensor(w, src_views[k], ps(bP, 0, 512), ALU.mult),
                 reads=list(okey) + [pk(bP)], writes=[("lnw",)], tag=f"{tag}lnw{k}")
            if affine is None:
                if v0_dst is not None:
                    p.op("vector", lambda e, k=k: e.tensor_tensor(v0_dst(k), w, ps(bQ, 0, 512), ALU.add),
                         reads=[("lnw",), pk(bQ)], writes=[v0_key(k)], tag=f"{tag}v0{k}")
                    p.op("vector", lambda e, k=k: e.tensor_copy(hn_dst(k), v0_dst(k)),
                         reads=[v0_key(k)], writes=[hn_key], tag=f"{tag}hn{k}")
                    v0_post(k)
                else:
                    p.op("vector", lambda e, k=k: e.tensor_tensor(hn_dst(k), w, ps(bQ, 0, 512), ALU.add),
                         reads=[("lnw",), pk(bQ)], writes=[hn_key], tag=f"{tag}hn{k}")
            else:
                lgc, lbc = affine
                p.op("vector", lambda e, k=k: e.tensor_tensor(w, w, ps(bQ, 0, 512), ALU.add),
                     reads=[("lnw",), pk(bQ)], writes=[("lnw",)], tag=f"{tag}lnw2{k}")
                p.op("vector", lambda e, k=k: e.tensor_scalar(
                        hn_dst(k), w, lgc[:, k:k + 1], lbc[:, k:k + 1], ALU.mult, ALU.add),
                     reads=[("lnw",), ("lgc",), ("lbc",)], writes=[hn_key], tag=f"{tag}hn{k}")


def bcast_free(apx, reps):
    import dataclasses
    a = list(apx.ap)
    return dataclasses.replace(apx, ap=type(apx.ap)([a[0], [0, reps], a[1]]))


def bcast_dram(apx, parts):
    import dataclasses
    return dataclasses.replace(apx, ap=type(apx.ap)([[0, parts]] + list(apx.ap)[1:]))



def pass_attn_qkv(kb, tag, lw, xin, guarded, vres_name, first_ln):
    cfg, p, io = kb.cfg, kb.prog, kb.io
    T, CH, NCH = cfg.T, cfg.CH, cfg.NCH
    ps, pk, psb16 = kb.ps, kb.pk, kb.psb16
    tg = tag

    kb.open_scope()
    L = type("L", (), {})()
    L.wqkvb = kb.salloc("wqkvb", [128, 8 * 1536], BF16)
    L.qsf = kb.salloc("qsf", [128, 512], F32)
    L.ksf = kb.salloc("ksf", [128, 512], F32)
    L.hbst = kb.salloc("hbst", [128, 8 * 513], BF16)
    L.lnxb = kb.salloc("lnxb", [128, 8 * 512], BF16)
    L.lnxsq = kb.salloc("lnxsq", [128, 8 * 512], BF16)
    L.qsq = kb.salloc("qsq", [128, 512], F32)
    L.rln = kb.salloc("rln", [128, 16], F32)
    L.uq = kb.salloc("uq", [128, 512], F32)
    L.qn = kb.salloc("qn", [128, 512], F32)
    L.qA = kb.salloc("qA", [128, 512], F32)
    L.qB = kb.salloc("qB", [128, 512], F32)
    L.qrot = kb.salloc("qrot", [128, 512], BF16)
    L.qkstg = {s: kb.salloc(f"qkstg{s}", [128, 4 * 512], BF16) for s in ("q", "k")}
    L.vstg = kb.salloc("vstg", [128, 4 * 8 * 65], BF16)
    L.tbl = {}
    for nm in ("csq", "snq", "csk", "snk"):
        L.tbl[nm] = kb.salloc(nm, [128, cfg.NT * 32], F32)
        p.dma("sync", L.tbl[nm][:], io[nm], "wq2", writes=[("c", "f" + nm)], tag=tg + "ld" + nm)

    p.dma("gpsimd", L.wqkvb[:].rearrange("p (a n) -> p a n", a=8), io[f"wqkv{lw}"].rearrange("(a p) n -> p a n", p=128),
          "wq", writes=[("wqkvb",)], tag=tg + "ldwqkv")
    p.dma("sync", L.qsf[:], bcast_dram(io[f"qsrow{lw}"], 128), "wq2",
          writes=[("qsf",)], tag=tg + "ldqs")
    p.dma("sync", L.ksf[:], bcast_dram(io[f"ksrow{lw}"], 128), "wq2",
          writes=[("ksf",)], tag=tg + "ldks")
    vstg3 = L.vstg[:].rearrange("p (t h x) -> p t h x", h=8, x=65)
    p.op("vector", lambda e: e.memset(vstg3[:, :, :, 64:65], 1.0),
         reads=[], writes=[("vstg",)], tag=tg + "vones")

    hb = L.hbst
    for c in range(NCH):
        if first_ln:
            xs = kb.xst[c % 2]
            p.dma("sync", xs[:, 0:8 * 512].rearrange("p (a n) -> p a n", a=8),
                  xin[:, c * CH:(c + 1) * CH].rearrange("(a p) n -> p a n", p=128),
                  f"xl{c % 2}", writes=[("xst", c % 2)], tag=tg + f"ldx{c}")
            src = [xs[:, k * 512:(k + 1) * 512] for k in range(8)]
            if c == 0:
                p.op("vector", lambda e: e.memset(
                    hb[:].rearrange("p (a n) -> p a n", n=513)[:, :, 0:1], 0.0),
                    reads=[("hb",)], writes=[("hb",)], tag=tg + "hb0")
            else:
                p.op("vector", lambda e: e.tensor_copy(
                    hb[:].rearrange("p (a n) -> p a n", n=513)[:, :, 0:1],
                    hb[:].rearrange("p (a n) -> p a n", n=513)[:, :, 512:513]),
                    reads=[("hb",)], writes=[("hb",)], tag=tg + f"hbt{c}")
            v0c = kb.vout

            def v0_post(k, c=c):
                p.dma("gpsimd", io[vres_name][k * 128:(k + 1) * 128, 1 + c * CH: 1 + (c + 1) * CH],
                      v0c[k % 2][:], f"vst{k % 2}", reads=[("voutbuf", k % 2)],
                      writes=[("v0", c, k)], tag=tg + f"stv0_{c}_{k}")
            kb.ln_chunk(L, src, [("xst", c % 2)],
                        hn_dst=lambda k: hb[:, k * 513 + 1: k * 513 + 513],
                        hn_key=("hb",), affine=None,
                        v0_dst=lambda k: v0c[k % 2][:, 0:512],
                        v0_key=lambda k: ("voutbuf", k % 2),
                        v0_post=v0_post, tag=tg + f"ln{c}_")
        else:
            p.dma("gpsimd", hb[:].rearrange("p (a n) -> p a n", n=513),
                  xin[:, c * CH: c * CH + 513].rearrange("(a p) n -> p a n", p=128),
                  "hbl", reads=[("hb",)], writes=[("hb",)], tag=tg + f"ldhb{c}")

        for tt in range(4):
            gt = c * 4 + tt
            bq, bk2, bv = (0, 1, 2) if tt % 2 == 0 else (4, 5, 6)

            def mmqkv(e, tt=tt, bq=bq, bk2=bk2, bv=bv):
                r = None
                for k in range(8):
                    off = 1 if k < 4 else 0
                    lhsT = hb[:, k * 513 + off + tt * 128: k * 513 + off + tt * 128 + 128]
                    st = (k == 0)
                    sp = (k == 7)
                    e.matmul(ps(bq, 0, 512), lhsT, L.wqkvb[:, k * 1536 + 0: k * 1536 + 512], start=st, stop=sp)
                    e.matmul(ps(bk2, 0, 512), lhsT, L.wqkvb[:, k * 1536 + 512: k * 1536 + 1024], start=st, stop=sp)
                    r = e.matmul(ps(bv, 0, 512), lhsT, L.wqkvb[:, k * 1536 + 1024: k * 1536 + 1536], start=st, stop=sp)
                return r
            p.op("tensor", mmqkv, reads=[("hb",), ("wqkvb",)],
                 writes=[pk(bq), pk(bk2), pk(bv)], tag=tg + f"qkv{c}_{tt}")

            p.op("scalar", lambda e, tt=tt, bv=bv: e.copy(
                    vstg3[:, tt, :, 0:64], ps(bv, 0, 512)[:].rearrange("p (h x) -> p h x", h=8)),
                 reads=[pk(bv)], writes=[("vstg",)], tag=tg + f"vev{c}_{tt}")

            for side, bqq, scl, cs, sn in (
                    ("q", bq, L.qsf, L.tbl["csq"], L.tbl["snq"]),
                    ("k", bk2, L.ksf, L.tbl["csk"], L.tbl["snk"])):
                sclkey = ("qsf",) if side == "q" else ("ksf",)
                p.op("scalar", lambda e, bqq=bqq: e.activation(L.qsq[:], ps(bqq, 0, 512), AF.Square),
                     reads=[pk(bqq)], writes=[("qsq",)], tag=tg + f"{side}sq{gt}")
                def ssred(e):
                    for h in range(8):
                        e.tensor_reduce(L.rln[:, h:h + 1], L.qsq[:, h * 64:(h + 1) * 64],
                                        AX.X, ALU.add)
                    return e.drain()
                p.op("vector", ssred, reads=[("qsq",)], writes=[("rln",)], tag=tg + f"{side}ss{gt}")
                p.op("vector", lambda e: e.tensor_scalar(L.rln[:, 8:16], L.rln[:, 0:8], 1e-20, None, ALU.max),
                     reads=[("rln",)], writes=[("rln",)], tag=tg + f"{side}mx{gt}")
                p.op("scalar", lambda e: e.activation(L.rln[:, 8:16], L.rln[:, 8:16], AF.Ln),
                     reads=[("rln",)], writes=[("rln",)], tag=tg + f"{side}ln{gt}")
                p.op("scalar", lambda e: e.activation(L.rln[:, 8:16], L.rln[:, 8:16], AF.Exp, scale=-0.5),
                     reads=[("rln",)], writes=[("rln",)], tag=tg + f"{side}ex{gt}")
                import os as _osd2
                if _osd2.environ.get("DBG_SCRATCH") and gt == 0 and side == "q" and tg.startswith("b0h0A0"):
                    p.dma("gpsimd", io["dbg1"], L.rln[:], "dbg", reads=[("rln",)],
                          writes=[("dbg",)], tag="dbgrln")
                    p.op("scalar", lambda e, bqq=bqq: e.copy(kb.lnw[:], ps(bqq, 0, 512)),
                         reads=[pk(bqq)], writes=[("dbgq",)], tag="dbgqps_cp")
                    p.dma("gpsimd", io["dbg2"], kb.lnw[:], "dbg", reads=[("dbgq",)],
                          writes=[("dbg",)], tag="dbgqps")
                    p.dma("gpsimd", io["dbg3"], L.qsq[:], "dbg", reads=[("qsq",)],
                          writes=[("dbg",)], tag="dbgqsq")
                p.op("vector", lambda e, bqq=bqq, scl=scl: e.tensor_tensor(L.uq[:], ps(bqq, 0, 512), scl[:], ALU.mult),
                     reads=[pk(bqq), sclkey], writes=[("uq",)], tag=tg + f"{side}u{gt}")
                qksc = QK_SCALE if side == "q" else None
                for h in range(8):
                    p.op("vector", lambda e, h=h, qksc=qksc: e.tensor_scalar(
                            L.qn[:, h * 64:(h + 1) * 64], L.uq[:, h * 64:(h + 1) * 64],
                            L.rln[:, 8 + h:9 + h], qksc,
                            ALU.mult, ALU.mult if qksc else Ellipsis) if qksc else
                            e.tensor_scalar(
                            L.qn[:, h * 64:(h + 1) * 64], L.uq[:, h * 64:(h + 1) * 64],
                            L.rln[:, 8 + h:9 + h], None, ALU.mult),
                         reads=[("uq",), ("rln",)], writes=[("qn",)], tag=tg + f"{side}n{gt}_{h}")
                csv = cs[:, gt * 32:(gt + 1) * 32]
                snv = sn[:, gt * 32:(gt + 1) * 32]
                qn3 = L.qn[:].rearrange("p (g x) -> p g x", x=32)
                cs3 = bcast_free(csv, 16)
                sn3 = bcast_free(snv, 16)
                ck = "csq" if side == "q" else "csk"
                sk2 = "snq" if side == "q" else "snk"
                p.op("vector", lambda e, cs3=cs3, qn3=qn3: e.tensor_tensor(
                        L.qA[:].rearrange("p (g x) -> p g x", x=32), qn3, cs3, ALU.mult),
                     reads=[("qn",), ("c", "f" + ck)], writes=[("qA",)], tag=tg + f"{side}A{gt}")
                p.op("vector", lambda e, sn3=sn3, qn3=qn3: e.tensor_tensor(
                        L.qB[:].rearrange("p (g x) -> p g x", x=32), qn3, sn3, ALU.mult),
                     reads=[("qn",), ("c", "f" + sk2)], writes=[("qB",)], tag=tg + f"{side}B{gt}")
                qa3 = L.qA[:].rearrange("p (h v x) -> p h v x", v=2, x=32)
                qb3 = L.qB[:].rearrange("p (h v x) -> p h v x", v=2, x=32)
                qr3 = L.qrot[:].rearrange("p (h v x) -> p h v x", v=2, x=32)
                p.op("vector", lambda e, qa3=qa3, qb3=qb3, qr3=qr3: e.tensor_tensor(
                        qr3[:, :, 0, :], qa3[:, :, 0, :], qb3[:, :, 1, :], ALU.subtract),
                     reads=[("qA",), ("qB",)], writes=[("qrot",)], tag=tg + f"{side}r1{gt}")
                p.op("vector", lambda e, qa3=qa3, qb3=qb3, qr3=qr3: e.tensor_tensor(
                        qr3[:, :, 1, :], qa3[:, :, 1, :], qb3[:, :, 0, :], ALU.add),
                     reads=[("qA",), ("qB",)], writes=[("qrot",)], tag=tg + f"{side}r2{gt}")

                def trans(e):
                    r = None
                    for j in range(4):
                        r = e.transpose(psb16(3, j * 256, 128), L.qrot[:, j * 128:(j + 1) * 128], kb.ident[:])
                    return r
                p.op("tensor", trans, reads=[("qrot",), ("c", "identc")],
                     writes=[pk(3)], tag=tg + f"{side}tr{gt}")
                for j in range(4):
                    p.op("scalar", lambda e, side=side, tt=tt, j=j: e.copy(
                            L.qkstg[side][:, j * 512 + tt * 128: j * 512 + tt * 128 + 128],
                            psb16(3, j * 256, 128)),
                         reads=[pk(3)], writes=[("qkstg", side)], tag=tg + f"{side}te{gt}_{j}")

        dstq = io["qfm"]
        dstk = io["kfm"]
        for j in range(4):
            p.dma("gpsimd", dstq[j * 128:(j + 1) * 128, c * CH:(c + 1) * CH],
                  L.qkstg["q"][:, j * 512:(j + 1) * 512], "qst",
                  reads=[("qkstg", "q")], writes=[("qfm", c, j)], tag=tg + f"stq{c}_{j}")
            p.dma("gpsimd", dstk[j * 128:(j + 1) * 128, c * CH:(c + 1) * CH],
                  L.qkstg["k"][:, j * 512:(j + 1) * 512], "kst",
                  reads=[("qkstg", "k")], writes=[("kfm", c, j)], tag=tg + f"stk{c}_{j}")
        p.dma("gpsimd", io["vtk"][:, c * 4 * 520:(c + 1) * 4 * 520],
              L.vstg[:], "vst2", reads=[("vstg",)], writes=[("vtk", c)], tag=tg + f"stv{c}")
    kb.close_scope()



def pass_attn_core(kb, tag, lw, vres_name, vres_guarded, vdst):
    cfg, p, io = kb.cfg, kb.prog, kb.io
    T, CH, NCH, NT = cfg.T, cfg.CH, cfg.NCH, cfg.NT
    ps, pk, psb16 = kb.ps, kb.pk, kb.psb16
    tg = tag

    kb.open_scope()
    L = type("L", (), {})()
    L.vtok = kb.salloc("vtok", [128, NT * 8 * 65], BF16)
    L.qh = [kb.salloc(f"qh{i}", [64, T], BF16) for i in range(2)]
    L.kh = [kb.salloc(f"kh{i}", [64, T], BF16) for i in range(2)]
    L.attnT = kb.salloc("attnT", [128, 1536], BF16)
    L.mcol = kb.salloc("mcol", [128, NT], F32)
    L.msc = kb.salloc("msc", [NT, 128], F32)
    L.mrow = kb.salloc("mrow", [1, T], BF16)
    L.invS = kb.salloc("invS", [128, 2], F32)
    L.otok = kb.salloc("otok", [128, NT * 512], BF16)
    L.ofm = kb.salloc("ofm", [128, 4 * 512], BF16)
    L.wob = kb.salloc("wob", [128, 4 * DM], BF16)

    p.dma("gpsimd", L.wob[:].rearrange("p (a n) -> p a n", a=4), io[f"wo{lw}"].rearrange("(a p) n -> p a n", p=128),
          "wo", writes=[("wob",)], tag=tg + "ldwo")
    p.dma("sync", L.vtok[:], io["vtk"], "vtl", writes=[("vtok",)], tag=tg + "ldvt")
    vtok3 = L.vtok[:].rearrange("p (t h x) -> p t h x", h=8, x=65)

    for h in range(HEADS):
        par = h % 2
        p.dma("sync", L.qh[par][:], io["qfm"][h * 64:(h + 1) * 64, :], f"qh{par}",
              reads=[("qh", par)], writes=[("qh", par)], tag=tg + f"ldq{h}")
        p.dma("sync", L.kh[par][:], io["kfm"][h * 64:(h + 1) * 64, :], f"kh{par}",
              reads=[("kh", par)], writes=[("kh", par)], tag=tg + f"ldk{h}")
        qv, kv = L.qh[par], L.kh[par]
        for qt in range(NT):
            start = max(0, qt * 128 - 512)
            n = qt * 128 + 128 - start
            bk0 = 0 if qt % 2 == 0 else 2

            def mmsim(e, qt=qt, start=start, n=n, bk0=bk0, qv=qv, kv=kv):
                lhsT = qv[:, qt * 128: qt * 128 + 128]
                r = None
                o = 0
                while o < n:
                    wdt = min(512, n - o)
                    r = e.matmul(ps(bk0, 0, 1024)[:, o:o + wdt], lhsT,
                                 kv[:, start + o: start + o + wdt],
                                 start=True, stop=True)
                    o += wdt
                return r
            p.op("tensor", mmsim, reads=[("qh", par), ("kh", par)],
                 writes=[pk(bk0), pk(bk0 + 1)], tag=tg + f"sim{h}_{qt}")
            p.op("vector", lambda e, qt=qt, n=n, bk0=bk0: e.tensor_reduce(
                    L.mcol[:, qt:qt + 1], ps(bk0, 0, 1024)[:, 0:n], AX.X, ALU.max),
                 reads=[pk(bk0), pk(bk0 + 1)], writes=[("mcol",)], tag=tg + f"max{h}_{qt}")
        p.op("tensor", lambda e: e.transpose(ps(4, 0, 128)[0:NT, :], L.mcol[:, 0:NT], kb.identf[:]),
             reads=[("mcol",), ("c", "fidentc")], writes=[pk(4)], tag=tg + f"mtr{h}")
        p.op("scalar", lambda e: e.copy(L.msc[0:NT, :], ps(4, 0, 128)[0:NT, :]),
             reads=[pk(4)], writes=[("msc",)], tag=tg + f"mte{h}")
        p.dma("gpsimd", L.mrow[0:1, :].rearrange("a (t p) -> a t p", p=128),
              L.msc[0:NT, :], "mrow", reads=[("msc",), ("mrow",)], writes=[("mrow",)],
              tag=tg + f"mrow{h}")

        for qc in range(T // 256):
            j0 = max(0, 4 - 2 * qc)
            nj = 6 - j0
            bk0 = 0 if qc % 2 == 0 else 4

            def mmsim2(e, qc=qc, j0=j0, bk0=bk0, qv=qv, kv=kv):
                r = None
                for j in range(j0, 6):
                    kstart = qc * 256 - 512 + j * 128
                    o = (j - j0) * 256
                    e.matmul(ps(bk0, 0, 2048)[:, o:o + 256],
                             kv[:, kstart: kstart + 128],
                             qv[:, qc * 256: qc * 256 + 256],
                             start=True, stop=False)
                    r = e.matmul(ps(bk0, 0, 2048)[:, o:o + 256],
                                 kb.negonesrow[0:1, :],
                                 L.mrow[0:1, qc * 256: qc * 256 + 256],
                                 start=False, stop=True)
                return r
            pks2 = [pk(bk0), pk(bk0 + 1), pk(bk0 + 2)]
            p.op("tensor", mmsim2, reads=[("qh", par), ("kh", par), ("mrow",), ("c", "negonesrow")],
                 writes=pks2, tag=tg + f"sim2_{h}_{qc}")
            p.op("scalar", lambda e, nj=nj, bk0=bk0: e.activation(
                    L.attnT[:, 0:nj * 256], ps(bk0, 0, 2048)[:, 0:nj * 256], AF.Exp),
                 reads=pks2, writes=[("attnT",)], tag=tg + f"exp{h}_{qc}")
            for j in (0, 1, 4, 5):
                if j < j0:
                    continue
                o = (j - j0) * 256
                p.op("vector", lambda e, o=o, j=j: e.tensor_tensor(
                        L.attnT[:, o:o + 256], L.attnT[:, o:o + 256], kb.m01[j][:], ALU.mult),
                     reads=[("attnT",), ("c", f"m01_{j}")], writes=[("attnT",)],
                     tag=tg + f"msk{h}_{qc}_{j}")
            for half in range(2):
                bav = 3 if half == 0 else 7

                def mmav(e, qc=qc, j0=j0, half=half, h=h, bav=bav):
                    r = None
                    for j in range(j0, 6):
                        ktg = qc * 2 - 4 + j
                        r = e.matmul(ps(bav, 0, 65),
                                     L.attnT[:, (j - j0) * 256 + half * 128: (j - j0) * 256 + half * 128 + 128],
                                     vtok3[:, ktg, h, :],
                                     start=(j == j0), stop=(j == 5))
                    return r
                p.op("tensor", mmav, reads=[("attnT",), ("vtok",)],
                     writes=[pk(bav)], tag=tg + f"av{h}_{qc}_{half}")
                p.op("vector", lambda e, bav=bav, half=half: e.reciprocal(
                        L.invS[:, half:half + 1], ps(bav, 64, 1)),
                     reads=[pk(bav)], writes=[("invS",)], tag=tg + f"is{h}_{qc}_{half}")
                p.op("scalar", lambda e, qc=qc, half=half, h=h, bav=bav: e.activation(
                        L.otok[:, (qc * 2 + half) * 512 + h * 64: (qc * 2 + half) * 512 + h * 64 + 64],
                        ps(bav, 0, 64), AF.Copy, scale=L.invS[:, half:half + 1]),
                     reads=[pk(bav), ("invS",)], writes=[("otok",)], tag=tg + f"oev{h}_{qc}_{half}")

    for c in range(NCH):
        xs = kb.xst[c % 2]
        p.dma("sync", xs[:, 0:8 * 512].rearrange("p (a n) -> p a n", a=8),
              io[vres_name][:, (1 if vres_guarded else 0) + c * CH:
                            (1 if vres_guarded else 0) + (c + 1) * CH]
              .rearrange("(a p) n -> p a n", p=128),
              f"xl{c % 2}", writes=[("xst", c % 2)], tag=tg + f"ldvr{c}")
        for tt in range(4):
            gt = c * 4 + tt

            def otr(e, gt=gt):
                r = None
                for j in range(4):
                    r = e.transpose(psb16(3, j * 256, 128),
                                    L.otok[:, gt * 512 + j * 128: gt * 512 + j * 128 + 128],
                                    kb.ident[:])
                return r
            p.op("tensor", otr, reads=[("otok",), ("c", "identc")], writes=[pk(3)],
                 tag=tg + f"otr{gt}")
            for j in range(4):
                p.op("scalar", lambda e, j=j, tt=tt: e.copy(
                        L.ofm[:, j * 512 + tt * 128: j * 512 + tt * 128 + 128], psb16(3, j * 256, 128)),
                     reads=[pk(3)], writes=[("ofm",)], tag=tg + f"ote{gt}_{j}")
        for m in range(8):
            bw = m % 2

            def mmwo(e, m=m, bw=bw):
                r = None
                for k in range(4):
                    r = e.matmul(ps(bw, 0, 512), L.wob[:, k * DM + m * 128: k * DM + m * 128 + 128],
                                 L.ofm[:, k * 512:(k + 1) * 512], start=(k == 0), stop=(k == 3))
                return r
            p.op("tensor", mmwo, reads=[("ofm",), ("wob",)], writes=[pk(bw)], tag=tg + f"wo{c}_{m}")
            p.op("vector", lambda e, m=m, bw=bw, xs=xs: e.tensor_tensor(
                    kb.vout[m % 2][:], ps(bw, 0, 512), xs[:, m * 512:(m + 1) * 512], ALU.add),
                 reads=[pk(bw), ("xst", c % 2)], writes=[("voutbuf", m % 2)], tag=tg + f"vo{c}_{m}")
            p.dma("gpsimd", vdst[:, 1 + c * CH: 1 + (c + 1) * CH]
                  .rearrange("(a p) n -> p a n", p=128)[:, m, :],
                  kb.vout[m % 2][:], f"vst{m % 2}", reads=[("voutbuf", m % 2)],
                  writes=[("vdst", c, m)], tag=tg + f"stv{c}_{m}")
    kb.close_scope()

def pass_mlp1(kb, tag, lw, vsrc_name):
    cfg, p, io = kb.cfg, kb.prog, kb.io
    T, CH, NCH = cfg.T, cfg.CH, cfg.NCH
    ps, pk = kb.ps, kb.pk
    tg = tag

    kb.open_scope()
    L = type("L", (), {})()
    L.w1b = kb.salloc("w1b", [128, 8 * W1P], BF16)
    L.b1c = kb.salloc("b1c", [128, 2 * NPAIR], F32)
    L.lgc = kb.salloc("lgc", [128, 8], F32)
    L.lbc = kb.salloc("lbc", [128, 8], F32)
    L.hnst = kb.salloc("hnst", [128, 8 * 512], BF16)
    L.glustg = kb.salloc("glustg", [128, 2 * 512], BF16)
    L.gelu_sb = kb.salloc("gelu", [128, 512], BF16)
    L.lnxb = kb.salloc("lnxb", [128, 8 * 512], BF16)
    L.lnxsq = kb.salloc("lnxsq", [128, 8 * 512], BF16)

    p.dma("gpsimd", L.w1b[:].rearrange("p (a n) -> p a n", a=8), io[f"w1{lw}"].rearrange("(a p) n -> p a n", p=128),
          "w1", writes=[("w1b",)], tag=tg + "ldw1")
    p.dma("sync", L.b1c[:], io[f"b1c{lw}"], "wmr", writes=[("b1c",)], tag=tg + "ldb1")
    p.dma("sync", L.lgc[:], io[f"lgc{lw}"], "wmr", writes=[("lgc",)], tag=tg + "ldlg")
    p.dma("sync", L.lbc[:], io[f"lbc{lw}"], "wmr", writes=[("lbc",)], tag=tg + "ldlb")

    for c in range(NCH):
        xs = kb.xst[c % 2]
        p.dma("sync", xs[:].rearrange("p (a n) -> p a n", n=513),
              io[vsrc_name][:, c * CH: c * CH + 513].rearrange("(a p) n -> p a n", p=128),
              f"xl{c % 2}", writes=[("xst", c % 2)], tag=tg + f"ldv{c}")
        sh = [xs[:, k * 513 + (1 if k < 4 else 0): k * 513 + (1 if k < 4 else 0) + 512]
              for k in range(8)]
        kb.ln_chunk(L, sh, [("xst", c % 2)],
                    hn_dst=lambda k: L.hnst[:, k * 512:(k + 1) * 512],
                    hn_key=("hnst",), affine=(L.lgc, L.lbc), tag=tg + f"ln{c}_")
        for t in range(NPAIR):
            ba, bg = (0, 1) if t % 2 == 0 else (4, 5)

            def mmw1(e, t=t, ba=ba, bg=bg):
                r = None
                for k in range(8):
                    rhs = L.hnst[:, k * 512:(k + 1) * 512]
                    e.matmul(ps(ba, 0, 512), L.w1b[:, k * W1P + (2 * t) * 128: k * W1P + (2 * t) * 128 + 128],
                             rhs, start=(k == 0), stop=(k == 7))
                    r = e.matmul(ps(bg, 0, 512), L.w1b[:, k * W1P + (2 * t + 1) * 128: k * W1P + (2 * t + 1) * 128 + 128],
                                 rhs, start=(k == 0), stop=(k == 7))
                return r
            p.op("tensor", mmw1, reads=[("hnst",), ("w1b",)], writes=[pk(ba), pk(bg)],
                 tag=tg + f"w1_{c}_{t}")
            p.op("scalar", lambda e, t=t, bg=bg: e.activation(
                    L.gelu_sb[:], ps(bg, 0, 512), AF.Gelu, bias=L.b1c[:, 2 * t + 1: 2 * t + 2]),
                 reads=[pk(bg), ("b1c",)], writes=[("gelu",)], tag=tg + f"gel{c}_{t}")
            p.op("vector", lambda e, t=t, ba=ba: e.scalar_tensor_tensor(
                    L.glustg[:, (t % 2) * 512: (t % 2) * 512 + 512], ps(ba, 0, 512),
                    L.b1c[:, 2 * t: 2 * t + 1], L.gelu_sb[:], ALU.add, ALU.mult),
                 reads=[pk(ba), ("gelu",), ("b1c",)], writes=[("glustg", t % 2)],
                 tag=tg + f"glu{c}_{t}")
            p.dma("gpsimd", io["glu"][:, (c * NPAIR + t) * 512: (c * NPAIR + t + 1) * 512],
                  L.glustg[:, (t % 2) * 512: (t % 2) * 512 + 512], f"gst{t % 2}",
                  reads=[("glustg", t % 2)], writes=[("glu", c, t)], tag=tg + f"stg{c}_{t}")
    kb.close_scope()



def pass_mlp2(kb, tag, lw, vsrc_name, vdst, final, xres, xres_guarded):
    cfg, p, io = kb.cfg, kb.prog, kb.io
    T, CH, NCH = cfg.T, cfg.CH, cfg.NCH
    ps, pk = kb.ps, kb.pk
    tg = tag

    kb.open_scope()
    L = type("L", (), {})()
    L.w2b = kb.salloc("w2b", [128, NPAIR * DM], BF16)
    L.b2r = kb.salloc("b2r", [1, DM], BF16)
    L.glu = kb.salloc("glu", [128, NPAIR * 512], BF16)
    L.x2st = kb.salloc("x2st", [128, 8 * 512], F32) if final else None

    p.dma("gpsimd", L.w2b[:].rearrange("p (a n) -> p a n", a=NPAIR), io[f"w2{lw}"].rearrange("(a p) n -> p a n", p=128),
          "w2", writes=[("w2b",)], tag=tg + "ldw2")
    p.dma("gpsimd", L.b2r[:], io[f"b2row{lw}"], "wmr2", writes=[("b2r",)], tag=tg + "ldb2")

    for c in range(NCH):
        xs = kb.xst[c % 2]
        p.dma("sync", xs[:].rearrange("p (a n) -> p a n", n=513),
              io[vsrc_name][:, c * CH: c * CH + 513].rearrange("(a p) n -> p a n", p=128),
              f"xl{c % 2}", writes=[("xst", c % 2)], tag=tg + f"ldv{c}")
        raw = [xs[:, k * 513 + 1: k * 513 + 513] for k in range(8)]
        p.dma("sync", L.glu[:], io["glu"][:, c * NPAIR * 512: (c + 1) * NPAIR * 512],
              "glul", reads=[("glub",)], writes=[("glub",)], tag=tg + f"ldg{c}")
        if final:
            p.dma("sync", L.x2st[:].rearrange("p (a n) -> p a n", n=512),
                  (xres[:, 1 + c * CH: 1 + (c + 1) * CH] if xres_guarded
                   else xres[:, c * CH: (c + 1) * CH]).rearrange("(a p) n -> p a n", p=128),
                  f"x2l{c % 2}", writes=[("x2st",)], tag=tg + f"ldx{c}")
        for m in range(8):
            bw = m % 2

            def mmw2(e, m=m, bw=bw):
                for k in range(NPAIR):
                    e.matmul(ps(bw, 0, 512), L.w2b[:, k * DM + m * 128: k * DM + m * 128 + 128],
                             L.glu[:, k * 512:(k + 1) * 512], start=(k == 0), stop=False)
                return e.matmul(ps(bw, 0, 512), L.b2r[0:1, m * 128: m * 128 + 128],
                                kb.onesrow[0:1, :], start=False, stop=True)
            p.op("tensor", mmw2, reads=[("glub",), ("w2b",), ("b2r",), ("c", "onesrow")],
                 writes=[pk(bw)], tag=tg + f"w2_{c}_{m}")
            if not final:
                p.op("vector", lambda e, m=m, bw=bw, raw=raw: e.tensor_tensor(
                        kb.vout[m % 2][:], ps(bw, 0, 512), raw[m], ALU.add),
                     reads=[pk(bw), ("xst", c % 2)], writes=[("voutbuf", m % 2)], tag=tg + f"vo{c}_{m}")
            else:
                p.op("vector", lambda e, m=m, raw=raw: e.scalar_tensor_tensor(
                        kb.lnw[:], raw[m], 0.5, L.x2st[:, m * 512:(m + 1) * 512], ALU.mult, ALU.add),
                     reads=[("xst", c % 2), ("x2st",)], writes=[("lnw",)], tag=tg + f"pre{c}_{m}")
                p.op("vector", lambda e, m=m, bw=bw: e.scalar_tensor_tensor(
                        kb.vout[m % 2][:], ps(bw, 0, 512), 0.5, kb.lnw[:], ALU.mult, ALU.add),
                     reads=[pk(bw), ("lnw",)], writes=[("voutbuf", m % 2)], tag=tg + f"xo{c}_{m}")
            col0 = (c * CH) if final else (1 + c * CH)
            p.dma("gpsimd", vdst[:, col0: col0 + CH].rearrange("(a p) n -> p a n", p=128)[:, m, :],
                  kb.vout[m % 2][:], f"vst{m % 2}", reads=[("voutbuf", m % 2)],
                  writes=[("vdst", c, m)], tag=tg + f"stv{c}_{m}")
    kb.close_scope()



def pass_mamba1(kb, tag, xin):
    cfg, p, io = kb.cfg, kb.prog, kb.io
    T, CH, NCH = cfg.T, cfg.CH, cfg.NCH
    ps, pk = kb.ps, kb.pk
    tg = tag

    kb.open_scope()
    L = type("L", (), {})()
    L.winb = kb.salloc("winb", [128, 8 * 2 * DI], BF16)
    L.cdiag = kb.salloc("cdiag", [128, DC * 16 * 128], BF16)
    L.wxb = kb.salloc("wxb", [128, 16 * 96], BF16)
    L.xcs = kb.salloc("xcs", [128, 16 * 515], BF16)
    L.hnst = kb.salloc("hnst", [128, 8 * 512], BF16)
    L.lnxb = kb.salloc("lnxb", [128, 8 * 512], BF16)
    L.lnxsq = kb.salloc("lnxsq", [128, 8 * 512], BF16)
    L.xsc = kb.salloc("xsc", [128, 16 * 512], BF16)
    L.szstg = kb.salloc("szstg", [128, 2 * 512], BF16)
    L.xdbst = kb.salloc("xdbst", [96, 512], F32)

    p.dma("gpsimd", L.winb[:].rearrange("p (a n) -> p a n", a=8), io["win"].rearrange("(a p) n -> p a n", p=128),
          "wmi", writes=[("winb",)], tag=tg + "ldwin")
    p.dma("gpsimd", L.cdiag[:].rearrange("p (a n) -> p a n", a=DC * 16), io["convdiag"].rearrange("(a p) n -> p a n", p=128),
          "wmc", writes=[("cdiag",)], tag=tg + "ldcd")
    p.dma("gpsimd", L.wxb[:].rearrange("p (a n) -> p a n", a=16), io["wx96"].rearrange("(a p) n -> p a n", p=128),
          "wmx", writes=[("wxb",)], tag=tg + "ldwx")

    xcs3 = L.xcs[:].rearrange("p (a n) -> p a n", n=515)
    for c in range(NCH):
        xs = kb.xst[c % 2]
        p.dma("sync", xs[:, 0:8 * 512].rearrange("p (a n) -> p a n", a=8),
              xin[:, c * CH:(c + 1) * CH].rearrange("(a p) n -> p a n", p=128),
              f"xl{c % 2}", writes=[("xst", c % 2)], tag=tg + f"ldx{c}")
        src = [xs[:, k * 512:(k + 1) * 512] for k in range(8)]
        kb.ln_chunk(L, src, [("xst", c % 2)],
                    hn_dst=lambda k: L.hnst[:, k * 512:(k + 1) * 512],
                    hn_key=("hnst",), affine=None, tag=tg + f"ln{c}_")
        if c == 0:
            p.op("vector", lambda e: e.memset(xcs3[:, :, 0:3], 0.0),
                 reads=[("xcs",)], writes=[("xcs",)], tag=tg + "xct0")
        else:
            p.op("vector", lambda e: e.tensor_copy(xcs3[:, :, 0:3], xcs3[:, :, 512:515]),
                 reads=[("xcs",)], writes=[("xcs",)], tag=tg + f"xct{c}")
        for m in range(32):
            bw = m % 2

            def mmwin(e, m=m, bw=bw):
                r = None
                for k in range(8):
                    r = e.matmul(ps(bw, 0, 512), L.winb[:, k * 2 * DI + m * 128: k * 2 * DI + m * 128 + 128],
                                 L.hnst[:, k * 512:(k + 1) * 512], start=(k == 0), stop=(k == 7))
                return r
            p.op("tensor", mmwin, reads=[("hnst",), ("winb",)], writes=[pk(bw)], tag=tg + f"wi{c}_{m}")
            if m < 16:
                p.op("scalar", lambda e, m=m, bw=bw: e.copy(xcs3[:, m, 3:515], ps(bw, 0, 512)),
                     reads=[pk(bw)], writes=[("xcs",)], tag=tg + f"xce{c}_{m}")
            else:
                mz = m - 16
                p.op("scalar", lambda e, mz=mz, bw=bw: e.activation(
                        L.szstg[:, (mz % 2) * 512: (mz % 2) * 512 + 512], ps(bw, 0, 512), AF.Silu),
                     reads=[pk(bw)], writes=[("szstg", mz % 2)], tag=tg + f"sze{c}_{mz}")
                p.dma("gpsimd", io["sz"][:, (c * 16 + mz) * 512: (c * 16 + mz + 1) * 512],
                      L.szstg[:, (mz % 2) * 512: (mz % 2) * 512 + 512], f"szst{mz % 2}",
                      reads=[("szstg", mz % 2)], writes=[("sz", c, mz)], tag=tg + f"szs{c}_{mz}")
        for ft in range(16):
            bc = 2 + ft % 2

            def mmcv(e, ft=ft, bc=bc):
                r = None
                for k in range(DC):
                    r = e.matmul(ps(bc, 0, 512), L.cdiag[:, (k * 16 + ft) * 128: (k * 16 + ft) * 128 + 128],
                                 xcs3[:, ft, k: k + 512], start=(k == 0), stop=(k == DC - 1))
                return r
            p.op("tensor", mmcv, reads=[("xcs",), ("cdiag",)], writes=[pk(bc)], tag=tg + f"cv{c}_{ft}")
            p.op("scalar", lambda e, ft=ft, bc=bc: e.activation(
                    L.xsc[:, ft * 512:(ft + 1) * 512], ps(bc, 0, 512), AF.Silu,
                    bias=kb.convb[:, ft: ft + 1]),
                 reads=[pk(bc), ("c", "fconvbcol")], writes=[("xsc", ft)], tag=tg + f"cvs{c}_{ft}")
            p.dma("gpsimd", io["xs"][:, (c * 16 + ft) * 512: (c * 16 + ft + 1) * 512],
                  L.xsc[:, ft * 512:(ft + 1) * 512], "xsst",
                  reads=[("xsc", ft)], writes=[("xs", c, ft)], tag=tg + f"xss{c}_{ft}")

        def mmwx(e):
            r = None
            for k in range(16):
                r = e.matmul(ps(4, 0, 512)[0:96, :], L.wxb[:, k * 96: k * 96 + 96],
                             L.xsc[:, k * 512:(k + 1) * 512], start=(k == 0), stop=(k == 15))
            return r
        p.op("tensor", mmwx, reads=[("xsc", k) for k in range(16)] + [("wxb",)],
             writes=[pk(4)], tag=tg + f"wx{c}")
        p.op("scalar", lambda e: e.copy(L.xdbst[:], ps(4, 0, 512)[0:96, :]),
             reads=[pk(4)], writes=[("xdbst",)], tag=tg + f"xdbe{c}")
        p.dma("gpsimd", io["xdb"][:, c * CH:(c + 1) * CH], L.xdbst[:], "xdst",
              reads=[("xdbst",)], writes=[("xdb", c)], tag=tg + f"xdbs{c}")
        p.dma("gpsimd", io["cbr"][:, c * CH:(c + 1) * CH], L.xdbst[64:80, :], "cbst",
              reads=[("xdbst",)], writes=[("cbr", c)], tag=tg + f"cbrs{c}")
    kb.close_scope()



def pass_mamba2(kb, tag, xin, xout, parts=127):
    cfg, p, io = kb.cfg, kb.prog, kb.io
    T, CH, NCH = cfg.T, cfg.CH, cfg.NCH
    ps, pk, psb16 = kb.ps, kb.pk, kb.psb16
    tg = tag

    kb.open_scope()
    L = type("L", (), {})()
    L.woutb = kb.salloc("woutb", [128, 16 * DM], BF16)
    L.xss = kb.salloc("xss", [128, 16 * 512], BF16)
    L.szs = kb.salloc("szs", [128, 16 * 512], BF16)
    L.ys = kb.salloc("ys", [128, 16 * 512], BF16)
    L.xstt = kb.salloc("xstt", [128, 16 * 512], BF16)
    L.dtbuf = kb.salloc("dtbuf", [128, 16 * 128], BF16)
    L.xdb = kb.salloc("xdb", [96, 512], F32)
    L.cbb = kb.salloc("cbb", [128, 16 * 512], BF16)
    L.ctbb = kb.salloc("ctbb", [128, 512], F32)
    L.cbs = kb.salloc("cbs", [16, 512], BF16)
    L.dts = kb.salloc("dts", [128, 8], F32)
    L.bts = kb.salloc("bts", [128, 4 * 16], F32)
    L.btst = kb.salloc("btst", [128, 4 * 16], F32)
    L.cums = kb.salloc("cums", [128, 4], F32)
    L.cumflat = kb.salloc("cumflat", [1, 512], F32)
    L.cumr4 = kb.salloc("cumr4", [4, 128], F32)
    L.ets = kb.salloc("ets", [16, 512], BF16)
    L.chat = kb.salloc("chat", [16, 512], BF16)
    L.decs = kb.salloc("decs", [16, 512], BF16)
    L.dect = kb.salloc("dect", [128, 4 * 16], BF16)
    L.bhat = kb.salloc("bhat", [128, 4 * 16], BF16)
    L.psb = kb.salloc("psb", [128, 128], F32)
    L.ptmp = kb.salloc("ptmp", [128, 128], BF16)
    L.pb4 = kb.salloc("pb4", [128, 4 * 128], BF16)
    L.hts = kb.salloc("hts", [16, DI], F32)
    L.htb = kb.salloc("htb", [16, DI], BF16)
    L.biass = kb.salloc("biass", [128, 16], F32)
    L.ctm = kb.salloc("ctm", [128, 128], F32)
    L.bqc = kb.salloc("bqc", [16, 2], F32)
    L.db16 = kb.salloc("db16", [128, 4], BF16)
    L.yw = kb.salloc("yw", [128, 512], F32)
    L.drelu = kb.salloc("drelu", [128, 4], F32)

    p.dma("gpsimd", L.woutb[:].rearrange("p (a n) -> p a n", a=16), io["wout"].rearrange("(a p) n -> p a n", p=128),
          "wmo", writes=[("woutb",)], tag=tg + "ldwout")
    p.op("vector", lambda e: e.memset(L.hts[:], 0.0), reads=[], writes=[("hts",)], tag=tg + "h0")
    p.op("vector", lambda e: e.memset(L.htb[:], 0.0), reads=[], writes=[("htb",)], tag=tg + "hb0")

    for c in range(NCH):
        xs = kb.xst[c % 2]
        p.dma("sync", xs[:, 0:8 * 512].rearrange("p (a n) -> p a n", a=8),
              xin[:, c * CH:(c + 1) * CH].rearrange("(a p) n -> p a n", p=128),
              f"xl{c % 2}", writes=[("xst", c % 2)], tag=tg + f"ldx{c}")
        p.dma("sync", L.xss[:], io["xs"][:, c * 16 * 512:(c + 1) * 16 * 512],
              "xsl", reads=[("xss",)], writes=[("xss",)], tag=tg + f"ldxs{c}")
        p.dma("sync", L.szs[:], io["sz"][:, c * 16 * 512:(c + 1) * 16 * 512],
              "szl", reads=[("szs",)], writes=[("szs",)], tag=tg + f"ldsz{c}")
        p.dma("sync", L.xdb[:], io["xdb"][:, c * CH:(c + 1) * CH],
              "xdl", reads=[("xdb",)], writes=[("xdb",)], tag=tg + f"ldxdb{c}")
        import dataclasses as _dc
        import os as _os4
        if int(_os4.environ.get("BCAST_DMA", "1")):
            _cbsrc = io["cbr"][:, c * CH:(c + 1) * CH]
            _cbsrc = _dc.replace(_cbsrc, ap=type(_cbsrc.ap)([[0, 128]] + list(_cbsrc.ap)))
            p.dma("gpsimd", L.cbb[:].rearrange("p (n t) -> p n t", n=16), _cbsrc,
                  "cbl", reads=[("cbb",)], writes=[("cbb",)], tag=tg + f"ldcbb{c}")

        def trdb(e):
            r = None
            for sc in range(4):
                e.transpose(ps(7, 0 + sc * 16, 16), L.xdb[0:16, sc * 128:(sc + 1) * 128],
                            kb.identf[0:16, 0:16])
                r = e.transpose(ps(6, 264 + sc * 16, 16),
                                L.xdb[32:48, sc * 128:(sc + 1) * 128], kb.identf[32:48, 32:48])
            return r
        p.op("tensor", trdb, reads=[("xdb",), ("c", "fidentc")], writes=[pk(6), pk(7)], tag=tg + f"trdb{c}")
        def dtev(e):
            r = None
            for sc in range(4):
                r = e.copy(L.dts[:, sc:sc + 1], ps(7, 0 + sc * 16, 1))
            return r
        p.op("scalar", dtev, reads=[pk(7)], writes=[("dts",)], tag=tg + f"dte{c}")
        p.op("scalar", lambda e: e.copy(L.bts[:], ps(6, 264, 64)),
             reads=[pk(6)], writes=[("bts",)], tag=tg + f"bte{c}")
        p.op("scalar", lambda e: e.activation(L.dts[:, 4:8], L.dts[:, 0:4], AF.Abs),
             reads=[("dts",)], writes=[("dsp",)], tag=tg + f"sp1{c}")
        p.op("scalar", lambda e: e.activation(L.dts[:, 4:8], L.dts[:, 4:8], AF.Exp, scale=-1.0),
             reads=[("dsp",)], writes=[("dsp",)], tag=tg + f"sp2{c}")
        p.op("scalar", lambda e: e.activation(L.dts[:, 4:8], L.dts[:, 4:8], AF.Ln, bias=1.0),
             reads=[("dsp",)], writes=[("dsp",)], tag=tg + f"sp3{c}")
        p.op("vector", lambda e: e.tensor_scalar(L.drelu[:], L.dts[:, 0:4], 0.0, None, ALU.max),
             reads=[("dts",)], writes=[("drelu",)], tag=tg + f"sp4{c}")
        p.op("vector", lambda e: e.tensor_tensor(L.dts[:, 4:8], L.dts[:, 4:8], L.drelu[:], ALU.add),
             reads=[("dsp",), ("drelu",)], writes=[("dsp",)], tag=tg + f"sp5{c}")
        for sc in range(4):
            p.op("vector", lambda e, sc=sc: e.tensor_scalar(
                    L.btst[:, sc * 16:(sc + 1) * 16], L.bts[:, sc * 16:(sc + 1) * 16],
                    L.dts[:, 4 + sc: 5 + sc], None, ALU.mult),
                 reads=[("bts",), ("dsp",)], writes=[("btst",)], tag=tg + f"btd{c}_{sc}")
        p.op("vector", lambda e: e.tensor_copy(L.db16[:], L.dts[:, 4:8]),
             reads=[("dsp",)], writes=[("db16",)], tag=tg + f"db16{c}")
        def cumg(e):
            e.matmul(ps(6, 328, 4), kb.tri[:], L.db16[:], start=True, stop=False)
            r = None
            for j in range(1, 4):
                r = e.matmul(ps(6, 328 + j, 4 - j), kb.ones128[:], L.db16[:, 0:4 - j],
                             start=False, stop=(j == 3))
            return r
        p.op("tensor", cumg, reads=[("db16",), ("c", "tric"), ("c", "ones128")],
             writes=[pk(6)], tag=tg + f"cum{c}")
        p.op("scalar", lambda e: e.copy(L.cums[:], ps(6, 328, 4)),
             reads=[pk(6)], writes=[("cums",)], tag=tg + f"cume{c}")
        p.op("tensor", lambda e: e.transpose(ps(6, 332, 128)[0:4, :], L.cums[:], kb.identf[:]),
             reads=[("cums",), ("c", "fidentc")], writes=[pk(6)], tag=tg + f"cumt{c}")
        p.op("scalar", lambda e: e.copy(L.cumr4[:], ps(6, 332, 128)[0:4, :]),
             reads=[pk(6)], writes=[("cumr4",)], tag=tg + f"cumte{c}")
        p.dma("sync", L.cumflat[0:1, :].rearrange("a (s p) -> a s p", p=128), L.cumr4[:],
              "cfl", reads=[("cumr4",), ("cumflat",)], writes=[("cumflat",)], tag=tg + f"cfl{c}")
        if not (parts & 2):
            continue
        import os as _os5
        if int(_os5.environ.get("BCAST_DMA", "1")):
            p.dma("gpsimd", io["cum"][0:1, c * CH:(c + 1) * CH], L.cumflat[0:1, :],
                  "cms", reads=[("cumflat",)], writes=[("cum", c)], tag=tg + f"cms{c}")
            import dataclasses as _dc2
            _cmsrc = io["cum"][0:1, c * CH:(c + 1) * CH]
            _cmsrc = _dc2.replace(_cmsrc, ap=type(_cmsrc.ap)([[0, 128]] + list(_cmsrc.ap)[1:]))
            p.dma("gpsimd", L.ctbb[:], _cmsrc, "cml",
                  reads=[("ctbb",), ("cum", c)], writes=[("ctbb",)], tag=tg + f"cml{c}")
        p.op("scalar", lambda e: e.activation(L.ets[:], L.ctbb[0:16, :], AF.Exp,
                                              scale=kb.acols[0:16, 0:1]),
             reads=[("ctbb",), ("Acol",)], writes=[("ets",)], tag=tg + f"et{c}")
        p.op("vector", lambda e: e.tensor_copy(L.cbs[:], L.xdb[64:80, :]),
             reads=[("xdb",)], writes=[("cbs",)], tag=tg + f"cbe{c}")
        p.op("vector", lambda e: e.tensor_tensor(L.chat[:], L.cbs[:], L.ets[:], ALU.mult),
             reads=[("ets",), ("cbs",)], writes=[("chat",)], tag=tg + f"chat{c}")
        p.op("vector", lambda e: e.tensor_tensor(L.bqc[0:16, 0:1], kb.acols[0:16, 0:1],
                                                 L.ctbb[0:16, 511:512], ALU.mult),
             reads=[("ctbb",), ("Acol",)], writes=[("bqc",)], tag=tg + f"bq{c}")
        p.op("scalar", lambda e: e.activation(L.decs[:], L.ctbb[0:16, :], AF.Exp,
                                              scale=kb.acols[0:16, 1:2], bias=L.bqc[0:16, 0:1]),
             reads=[("ctbb",), ("bqc",), ("negA",)], writes=[("decs",)], tag=tg + f"dec{c}")

        def trdec(e):
            r = None
            for sc in range(4):
                r = e.transpose(psb16(6, 928 + sc * 16, 16), L.decs[:, sc * 128:(sc + 1) * 128], kb.ident[0:16, 0:16])
            return r
        p.op("tensor", trdec, reads=[("decs",), ("c", "identc")], writes=[pk(6)], tag=tg + f"trdc{c}")
        p.op("scalar", lambda e: e.copy(L.dect[:], psb16(6, 928, 64)),
             reads=[pk(6)], writes=[("dect",)], tag=tg + f"dce{c}")
        p.op("vector", lambda e: e.tensor_tensor(L.bhat[:], L.dect[:], L.btst[:], ALU.mult),
             reads=[("dect",), ("btst",)], writes=[("bhat",)], tag=tg + f"bhat{c}")
        p.op("scalar", lambda e: e.activation(L.bqc[0:16, 1:2], L.bqc[0:16, 0:1], AF.Exp),
             reads=[("bqc",)], writes=[("eq",)], tag=tg + f"eq{c}")
        p.op("scalar", lambda e: e.activation(L.hts[:], L.hts[:], AF.Copy,
                                              scale=L.bqc[0:16, 1:2]),
             reads=[("hts",), ("eq",)], writes=[("hts",)], tag=tg + f"hdec{c}")

        if not (parts & 4):
            continue
        import os as _os2
        scp = int(_os2.environ.get("SC_PARTS", "31"))
        for sc in range(4):
            if scp & 1:
                p.op("vector", lambda e, sc=sc: e.tensor_tensor(
                        L.ctm[:], L.ctbb[:, sc * 128:(sc + 1) * 128], kb.maskhuge[:], ALU.add),
                     reads=[("ctbb",), ("c", "fmaskhuge")], writes=[("ctm",)], tag=tg + f"ctm{c}_{sc}")
            if scp & 2:
                p.op("vector", lambda e, sc=sc: e.tensor_scalar(
                        L.biass[:], kb.nabc[:], L.cums[:, sc:sc + 1], None, ALU.mult),
                     reads=[("nabc",), ("cums",)], writes=[("biass",)], tag=tg + f"bias{c}_{sc}")
            def dtgen(e):
                r = None
                for n in range(16):
                    r = e.activation(L.dtbuf[:, n * 128:(n + 1) * 128], L.ctm[:], AF.Exp,
                                     scale=kb.abc[:, n:n + 1], bias=L.biass[:, n:n + 1])
                return r
            if scp & 4:
                p.op("scalar", dtgen, reads=[("ctm",), ("biass",), ("abc",)],
                     writes=[("dt",)], tag=tg + f"dt{c}_{sc}")

            def dbgen(e, sc=sc):
                r = None
                for n in range(16):
                    if kb.cfg.paranoid:
                        e.drain()
                    r = e.tensor_scalar(L.dtbuf[:, n * 128:(n + 1) * 128],
                                        L.dtbuf[:, n * 128:(n + 1) * 128],
                                        L.btst[:, sc * 16 + n: sc * 16 + n + 1], None, ALU.mult)
                return r
            if scp & 8:
                p.op("vector", dbgen, reads=[("dt",), ("btst",)], writes=[("dt",)],
                     tag=tg + f"db{c}_{sc}")

            def pacc(e, sc=sc):
                for n in range(16):
                    cslice = L.cbb[:, n * 512 + sc * 128: n * 512 + sc * 128 + 128]
                    if kb.cfg.paranoid:
                        e.drain()
                    if n == 0:
                        e.tensor_tensor(L.psb[:], L.dtbuf[:, 0:128], cslice, ALU.mult)
                    else:
                        e.tensor_tensor(L.ptmp[:], L.dtbuf[:, n * 128:(n + 1) * 128], cslice, ALU.mult)
                        if kb.cfg.paranoid:
                            e.drain()
                        e.tensor_tensor(L.psb[:], L.psb[:], L.ptmp[:], ALU.add)
                if kb.cfg.paranoid:
                    e.drain()
                return e.tensor_copy(L.pb4[:, sc * 128:(sc + 1) * 128], L.psb[:])
            if scp & 16:
                p.op("vector", pacc, reads=[("dt",), ("cbb",)], writes=[("P",), ("pb4",)],
                     tag=tg + f"pacc{c}_{sc}")
            for g in range(2 if (parts & 8) else 0):
                def trx(e, sc=sc, g=g):
                    r = None
                    for i in range(8):
                        dt = g * 8 + i
                        tb = 2 + i // 4
                        r = e.transpose(psb16(tb, (i % 4) * 128, 128),
                                        L.xss[:, dt * 512 + sc * 128: dt * 512 + sc * 128 + 128],
                                        kb.ident[:])
                    return r
                p.op("tensor", trx, reads=[("xss",), ("c", "identc")],
                     writes=[pk(2), pk(3)], tag=tg + f"xst{c}_{sc}_{g}")

                def cpx(e, sc=sc, g=g):
                    r = None
                    for i in range(8):
                        dt = g * 8 + i
                        tb = 2 + i // 4
                        r = e.copy(L.xstt[:, dt * 512 + sc * 128: dt * 512 + sc * 128 + 128],
                                   psb16(tb, (i % 4) * 128, 128))
                    return r
                p.op("scalar", cpx, reads=[pk(2), pk(3)], writes=[("xstt",)],
                     tag=tg + f"xse{c}_{sc}_{g}")

        for dt in range(16 if (parts & 16) else 0):
            by = 6 + dt % 2

            def mmy(e, dt=dt, by=by):
                r = None
                for sc in range(4):
                    e.matmul(ps(by, sc * 128, 128), L.xstt[:, dt * 512 + sc * 128: dt * 512 + sc * 128 + 128],
                             L.pb4[:, sc * 128:(sc + 1) * 128], start=True, stop=False)
                    r = e.matmul(ps(by, sc * 128, 128), L.htb[0:16, dt * 128:(dt + 1) * 128],
                                 L.chat[0:16, sc * 128:(sc + 1) * 128], start=False, stop=True)
                return r
            p.op("tensor", mmy, reads=[("xstt",), ("pb4",), ("htb",), ("chat",)],
                 writes=[pk(by)], tag=tg + f"y{c}_{dt}")
            p.op("vector", lambda e, dt=dt: e.tensor_scalar(
                    L.yw[:], L.xss[:, dt * 512:(dt + 1) * 512], kb.dcol[:, dt:dt + 1], None, ALU.mult),
                 reads=[("xss",), ("c", "fdcol")], writes=[("yu",)], tag=tg + f"yu{c}_{dt}")
            p.op("vector", lambda e, dt=dt, by=by: e.tensor_tensor(
                    L.yw[:], ps(by, 0, 512), L.yw[:], ALU.add),
                 reads=[pk(by), ("yu",)], writes=[("yu",)], tag=tg + f"yw{c}_{dt}")
            p.op("vector", lambda e, dt=dt: e.tensor_tensor(
                    L.ys[:, dt * 512:(dt + 1) * 512], L.yw[:], L.szs[:, dt * 512:(dt + 1) * 512], ALU.mult),
                 reads=[("yu",), ("szs",)], writes=[("ys", dt)], tag=tg + f"yo{c}_{dt}")

        for dt in range(16 if (parts & 32) else 0):
            def mmh(e, dt=dt):
                r = None
                for sc in range(4):
                    r = e.matmul(ps(4, 256, 128)[0:16, :], L.bhat[:, sc * 16:(sc + 1) * 16],
                                 L.xstt[:, dt * 512 + sc * 128: dt * 512 + sc * 128 + 128],
                                 start=(sc == 0), stop=(sc == 3))
                return r
            p.op("tensor", mmh, reads=[("bhat",), ("xstt",)], writes=[pk(4)], tag=tg + f"hmm{c}_{dt}")
            p.op("vector", lambda e, dt=dt: e.tensor_tensor(
                    L.hts[0:16, dt * 128:(dt + 1) * 128], ps(4, 256, 128)[0:16, :],
                    L.hts[0:16, dt * 128:(dt + 1) * 128], ALU.add),
                 reads=[pk(4), ("hts",)], writes=[("hts",)], tag=tg + f"hup{c}_{dt}")
        if parts & 32:
            p.op("vector", lambda e: e.tensor_copy(L.htb[:], L.hts[:]),
                 reads=[("hts",)], writes=[("htb",)], tag=tg + f"htb{c}")

        import os as _os3
        wop = int(_os3.environ.get("WOUT_PARTS", "7"))
        for m in range(8 if (parts & 64) else 0):
            bw = m % 2

            def mmo(e, m=m, bw=bw):
                r = None
                for k in range(16):
                    r = e.matmul(ps(bw, 0, 512), L.woutb[:, k * DM + m * 128: k * DM + m * 128 + 128],
                                 L.ys[:, k * 512:(k + 1) * 512], start=(k == 0), stop=(k == 15))
                return r
            if wop & 1:
                p.op("tensor", mmo, reads=[("ys", k) for k in range(16)] + [("woutb",)],
                     writes=[pk(bw)], tag=tg + f"woc{c}_{m}")
            if wop & 2:
                p.op("vector", lambda e, m=m, bw=bw, xs=xs: e.tensor_tensor(
                        kb.vout[m % 2][:], ps(bw, 0, 512), xs[:, m * 512:(m + 1) * 512], ALU.add),
                     reads=[pk(bw), ("xst", c % 2)], writes=[("voutbuf", m % 2)], tag=tg + f"x2o{c}_{m}")
            if wop & 4:
                p.dma("gpsimd", xout[:, c * CH:(c + 1) * CH].rearrange("(a p) n -> p a n", p=128)[:, m, :],
                      kb.vout[m % 2][:], f"vst{m % 2}", reads=[("voutbuf", m % 2)],
                      writes=[("x2o", c, m)], tag=tg + f"stx{c}_{m}")
    kb.close_scope()



def build(cfg, stop=None):
    kb = KB(cfg)
    p, io = kb.prog, kb.io
    step = [0]

    def go():
        step[0] += 1
        return stop is None or step[0] <= stop
    for b in range(cfg.NB):
        xb = io["x0"][b]
        ob = io["out"][b]
        for v in ("vA", "vB", "vC"):
            p.op("vector", lambda e: e.memset(kb.zc[:, 0:8], 0.0), reads=[], writes=[("z",)],
                 tag=f"b{b}z{v}")
            p.dma("gpsimd", io[v][:, 0:1].rearrange("(a p) n -> p a n", p=128),
                  kb.zc[:, 0:8].rearrange("p (a n) -> p a n", n=1), "vst0", reads=[("z",)], writes=[(v, -1)],
                  allow_slow_non_contiguous=True, tag=f"b{b}zg{v}")
        p.barrier()
        for half, xi, xo in ((0, xb, io["x1"]), (1, io["x2"], ob)):
            t = f"b{b}h{half}"
            if go():
                pass_attn_qkv(kb, t + "A0", 0, xi, False, "vA", first_ln=True)
                p.barrier()
            if go():
                pass_attn_core(kb, t + "B0", 0, "vA", True, io["vB"])
                p.barrier()
            if go():
                pass_mlp1(kb, t + "C0", 0, "vB")
                p.barrier()
            if go():
                pass_mlp2(kb, t + "D0", 0, "vB", io["vC"], False, None, False)
                p.barrier()
            if go():
                pass_attn_qkv(kb, t + "A1", 1, io["vC"], True, "vC", first_ln=False)
                p.barrier()
            if go():
                pass_attn_core(kb, t + "B1", 1, "vC", True, io["vA"])
                p.barrier()
            if go():
                pass_mlp1(kb, t + "C1", 1, "vA")
                p.barrier()
            if go():
                pass_mlp2(kb, t + "D1", 1, "vA", xo, True, xi, False)
                p.barrier()
            if half == 0:
                if go():
                    pass_mamba1(kb, t + "M1", io["x1"])
                    p.barrier()
                if go():
                    import os as _os
                    pass_mamba2(kb, t + "M2", io["x1"], io["x2"],
                                parts=int(_os.environ.get("MB2_PARTS", "127")))
                    p.barrier()
    stats = p.emit()
    return kb, stats



def kernel(**inputs):
    cfg = Cfg(T=4096, NB=2)
    consts = host_consts(cfg)
    weights = host_weights(inputs)
    x = np.ascontiguousarray(np.transpose(np.asarray(inputs["L0_attn_x"]), (0, 2, 1)))
    shards = [np.ascontiguousarray(x[2 * i:2 * i + 2]) for i in range(8)]
    kb, stats = build(cfg)
    shared = {**consts, **weights}
    in_maps = [dict(shared, x0=s) for s in shards]
    res = run_bass_kernel_spmd(kb.nc, in_maps, core_ids=list(range(8)))
    outs = [np.asarray(r["out"]) for r in res.results]
    full = np.concatenate(outs, axis=0)
    return np.ascontiguousarray(np.transpose(full, (0, 2, 1)))


# version 04
# speedup vs baseline: 1.0014x; 1.0014x over previous
import math
import numpy as np

import concourse.bass as bass
import concourse.mybir as mybir
from concourse.bass_utils import run_bass_kernel_spmd

F32 = mybir.dt.float32
BF16 = mybir.dt.bfloat16
AF = mybir.ActivationFunctionType
ALU = mybir.AluOpType
AX = mybir.AxisListType

DM = 1024
INNER = 512
HEADS = 8
DH = 64
WIN = 512
HID = 2730
HIDP = 2816
W1P = 2 * HIDP
NPAIR = HIDP // 128
DI = 2048
DS = 16
DC = 4
QK_SCALE = 8.0

ENGS = ("tensor", "vector", "scalar", "gpsimd", "sync")

BULK_SEMS = {"ldc", "ldcf", "wq", "wq2", "wo", "w1", "w2", "wmr", "wmr2",
             "wmi", "wmc", "wmx", "wmo", "vtl"}


class Cfg:
    def __init__(self, T=4096, NB=2, paranoid=False):
        assert T % 512 == 0
        self.T = T
        self.NB = NB
        self.CH = 512
        self.NCH = T // self.CH
        self.NT = T // 128
        self.paranoid = paranoid



class _Op:
    __slots__ = ("idx", "eng", "fn", "reads", "writes", "dma", "tag",
                 "waits", "pre_drain", "milestone", "bar")

    def __init__(self, idx, eng, fn, reads, writes, dma, tag):
        self.idx = idx
        self.eng = eng
        self.fn = fn
        self.reads = reads
        self.writes = writes
        self.dma = dma
        self.tag = tag
        self.waits = []
        self.pre_drain = False
        self.milestone = False
        self.bar = False


class Prog:
    def __init__(self, nc, paranoid=False):
        self.nc = nc
        self.paranoid = paranoid
        self.ops = []
        self.res_w = {}
        self.res_r = {}
        self.dma_tot = {}
        self.know = {e: {} for e in ENGS}
        self.snap = {}
        self.drained_upto = {e: -1 for e in ENGS}

    def op(self, eng, fn, reads=(), writes=(), tag=""):
        return self._rec(eng, fn, reads, writes, None, tag)

    def dma(self, eng, out, in_, semkey, reads=(), writes=(), tag="", **kw):
        if out.dtype != in_.dtype:
            assert eng == "gpsimd", f"cast dma must be gpsimd: {tag}"

        def fn(e, out=out, in_=in_, kw=kw):
            return e.dma_start(out=out, in_=in_, **kw)
        return self._rec(eng, fn, reads, writes, semkey, tag)

    def _rec(self, eng, fn, reads, writes, semkey, tag):
        idx = len(self.ops)
        o = _Op(idx, eng, fn, tuple(reads), tuple(writes), semkey, tag)
        self.ops.append(o)
        know = self.know[eng]

        def need(prod):
            if prod is None:
                return
            if prod[0] == "E":
                _, peng, pidx = prod
                if peng == eng:
                    if self.paranoid and semkey is None and self.drained_upto[eng] < pidx:
                        o.pre_drain = True
                        self.drained_upto[eng] = idx - 1
                    return
                if know.get(("E", peng), -1) >= pidx:
                    return
                o.waits.append(prod)
                know[("E", peng)] = pidx
                s = self.snap.get((peng, pidx))
                if s:
                    for k, v in s.items():
                        if know.get(k, -1) < v:
                            know[k] = v
            else:
                _, sk, cnt = prod
                if sk in BULK_SEMS:
                    cnt = self.dma_tot[sk]
                if know.get(("D", sk), -1) >= cnt:
                    return
                o.waits.append(("D", sk, cnt))
                know[("D", sk)] = cnt

        for r in reads:
            need(self.res_w.get(r))
        for w in writes:
            need(self.res_w.get(w))
            for rd in self.res_r.get(w, ()):
                need(rd)

        if semkey is not None:
            self.dma_tot[semkey] = self.dma_tot.get(semkey, 0) + 1
            me = ("D", semkey, self.dma_tot[semkey])
        else:
            me = ("E", eng, idx)
            self.snap[(eng, idx)] = dict(know)
        for w in writes:
            self.res_w[w] = me
            self.res_r[w] = []
        for r in reads:
            self.res_r.setdefault(r, []).append(me)
        return o

    def barrier(self):
        idx = len(self.ops)
        for e in ENGS:
            o = _Op(idx, e, None, (), (), None, "barrier")
            o.bar = True
            self.ops.append(o)
            idx += 1
        last = {}
        for o in self.ops:
            if not o.bar and o.dma is None:
                last[o.eng] = o.idx
        for e in ENGS:
            k = self.know[e]
            for e2, li in last.items():
                if e2 != e and k.get(("E", e2), -1) < li:
                    k[("E", e2)] = li
            for sk, c in self.dma_tot.items():
                if k.get(("D", sk), -1) < c:
                    k[("D", sk)] = c

    def emit(self):
        nc = self.nc
        for o in self.ops:
            for w in o.waits:
                if w[0] == "E":
                    self.ops[w[2]].milestone = True
        i = 0
        while i < len(self.ops):
            if self.ops[i].bar:
                last = {}
                dtot = {}
                for o in self.ops[:i]:
                    if o.bar:
                        continue
                    if o.dma is not None:
                        dtot[o.dma] = dtot.get(o.dma, 0) + 1
                    else:
                        last[o.eng] = o.idx
                for j, e in enumerate(ENGS):
                    b = self.ops[i + j]
                    b.waits = [("E", e2, li) for e2, li in last.items() if e2 != e]
                    b.waits += [("D", sk, c) for sk, c in dtot.items()]
                for e2, li in last.items():
                    self.ops[li].milestone = True
                i += len(ENGS)
            else:
                i += 1

        esem = {e: nc.alloc_semaphore(f"es_{e}") for e in ENGS}
        dsem = {sk: nc.alloc_semaphore(f"ds_{sk}") for sk in self.dma_tot}
        wv = {}
        run = {e: 0 for e in ENGS}
        for o in self.ops:
            if o.bar or o.dma is not None:
                continue
            if o.milestone:
                run[o.eng] += 1
            wv[o.idx] = run[o.eng]

        byeng = {e: [o for o in self.ops if o.eng == e] for e in ENGS}
        stats = {e: len(byeng[e]) for e in ENGS}

        with nc.Block() as block:
            def mk(e_name):
                def body(eng):
                    for o in byeng[e_name]:
                        seen = {}
                        for w in o.waits:
                            if w[0] == "E":
                                k = ("E", w[1])
                                v = wv[w[2]]
                            else:
                                k = ("D", w[1])
                                v = 16 * w[2]
                            if seen.get(k, -1) < v:
                                seen[k] = v
                        for (kind, name), v in seen.items():
                            eng.wait_ge(esem[name] if kind == "E" else dsem[name], v)
                        if o.bar:
                            continue
                        if o.pre_drain:
                            eng.drain()
                        inst = o.fn(eng)
                        if o.dma is not None:
                            inst.then_inc(dsem[o.dma], 16)
                        elif o.milestone:
                            if inst is None:
                                raise RuntimeError(f"milestone op returned None: {o.tag}")
                            inst.then_inc(esem[o.eng], 1)
                return body

            block.sync(mk("sync"))
            block.tensor(mk("tensor"))
            block.vector(mk("vector"))
            block.scalar(mk("scalar"))
            block.gpsimd(mk("gpsimd"))
        return stats



def host_consts(cfg):
    T = cfg.T
    c = {}
    c["identc"] = np.eye(128, dtype=np.float32)
    c["tric"] = np.triu(np.ones((128, 128), dtype=np.float32))
    c["ones128"] = np.ones((128, 128), dtype=np.float32)
    en = np.zeros((16, 16 * 128), dtype=np.float32)
    for n in range(16):
        en[n, n * 128:(n + 1) * 128] = 1.0
    c["ensel"] = en
    c["onesrow"] = np.ones((1, 512), dtype=np.float32)
    c["negonesrow"] = -np.ones((1, 128), dtype=np.float32)
    c["onescol"] = np.ones((128, 1), dtype=np.float32)
    i = np.arange(128)
    c["maskhuge"] = np.where(i[:, None] > i[None, :], 1e30, 0.0).astype(np.float32)
    r = np.arange(128)[:, None]
    q = np.arange(256)[None, :]
    for j in (0, 1, 4, 5):
        d = j * 128 + r - q
        c[f"m01_{j}"] = ((d >= 0) & (d <= WIN)).astype(np.float32)
    c["cpool"] = np.tile(np.array([[1e-5, 1e-20, 1.0, 0.0]], dtype=np.float32), (128, 1))

    half = np.arange(0, DH, 2, dtype=np.float64)
    inv_freq = 1.0 / 10000.0 ** (half / DH)
    t = np.arange(T, dtype=np.float64)
    freqs = t[:, None] * inv_freq[None, :]
    base = (half + 0.4 * DH) / (1.4 * DH)
    power = (t - T // 2) / float(WIN // 2)
    scale = base[None, :] ** power[:, None]
    cos = np.cos(freqs)
    sin = np.sin(freqs)

    def tokmaj(a):
        nt = T // 128
        return np.ascontiguousarray(
            a.reshape(nt, 128, 32).transpose(1, 0, 2).reshape(128, nt * 32)
        ).astype(np.float32)

    c["csq"] = tokmaj(cos * scale)
    c["snq"] = tokmaj(sin * scale)
    c["csk"] = tokmaj(cos / scale)
    c["snk"] = tokmaj(sin / scale)
    return c


def host_weights(inp):
    w = {}
    for l in range(2):
        w[f"wqkv{l}"] = np.ascontiguousarray(inp["L0_ffn_wqkv"][l])
        w[f"wo{l}"] = np.ascontiguousarray(inp["L0_ffn_wo"][l])
        w1 = inp["L0_ffn_w1"][l]
        a = np.zeros((DM, HIDP), dtype=np.float32)
        g = np.zeros((DM, HIDP), dtype=np.float32)
        a[:, :HID] = w1[:, :HID]
        g[:, :HID] = w1[:, HID:]
        w1p = np.zeros((DM, W1P), dtype=np.float32)
        for tt in range(NPAIR):
            w1p[:, (2 * tt) * 128:(2 * tt + 1) * 128] = a[:, tt * 128:(tt + 1) * 128]
            w1p[:, (2 * tt + 1) * 128:(2 * tt + 2) * 128] = g[:, tt * 128:(tt + 1) * 128]
        w[f"w1{l}"] = w1p
        b1 = inp["L0_ffn_b1"][l]
        b1c = np.zeros((128, 2 * NPAIR), dtype=np.float32)
        for tt in range(NPAIR):
            sl = slice(tt * 128, (tt + 1) * 128)
            apart = np.zeros(128, dtype=np.float32)
            gpart = np.zeros(128, dtype=np.float32)
            asrc = b1[:HID][sl]
            gsrc = b1[HID:][sl]
            apart[:len(asrc)] = asrc
            gpart[:len(gsrc)] = gsrc
            b1c[:, 2 * tt] = apart
            b1c[:, 2 * tt + 1] = gpart
        w[f"b1c{l}"] = b1c
        w2 = inp["L0_ffn_w2"][l]
        w2p = np.zeros((HIDP, DM), dtype=np.float32)
        w2p[:HID, :] = w2
        w[f"w2{l}"] = w2p
        w[f"b2row{l}"] = np.ascontiguousarray(inp["L0_ffn_b2"][l].reshape(1, DM))
        w[f"lgc{l}"] = np.ascontiguousarray(inp["L0_ffn_ln_g"][l].reshape(8, 128).T)
        w[f"lbc{l}"] = np.ascontiguousarray(inp["L0_ffn_ln_b"][l].reshape(8, 128).T)
        w[f"qsrow{l}"] = np.ascontiguousarray(
            np.tile(inp["L0_ffn_q_scale"][l].reshape(1, DH), (1, HEADS)))
        w[f"ksrow{l}"] = np.ascontiguousarray(
            np.tile(inp["L0_ffn_k_scale"][l].reshape(1, DH), (1, HEADS)))
    w["win"] = np.ascontiguousarray(inp["L0_attn_W_in"])
    w["wout"] = np.ascontiguousarray(inp["L0_attn_W_out"])
    wx = inp["L0_attn_W_x"]
    wx96 = np.zeros((DI, 96), dtype=np.float32)
    wx96[:, 0] = wx[:, 0]
    wx96[:, 32:48] = wx[:, 1:17]
    wx96[:, 64:80] = wx[:, 17:33]
    w["wx96"] = wx96
    cw = inp["L0_attn_conv_w"][:, 0, :]
    diag = np.zeros((DC, 16, 128, 128), dtype=np.float32)
    for k in range(DC):
        for ft in range(16):
            np.fill_diagonal(diag[k, ft], cw[ft * 128:(ft + 1) * 128, k])
    w["convdiag"] = np.ascontiguousarray(diag.reshape(DC * 16 * 128, 128))
    w["convbcol"] = np.ascontiguousarray(inp["L0_attn_conv_b"].reshape(16, 128).T)
    w["dcol"] = np.ascontiguousarray(inp["L0_attn_D"].reshape(16, 128).T)
    w["alogcol"] = np.ascontiguousarray(inp["L0_attn_A_log"].reshape(16, 1))
    return w


WSHAPES = {
    "win": [DM, 2 * DI], "wout": [DI, DM], "wx96": [DI, 96],
    "convdiag": [DC * 16 * 128, 128], "convbcol": [128, 16], "dcol": [128, 16],
    "alogcol": [16, 1],
}
for _l in range(2):
    WSHAPES.update({
        f"wqkv{_l}": [DM, 3 * INNER], f"wo{_l}": [INNER, DM],
        f"w1{_l}": [DM, W1P], f"b1c{_l}": [128, 2 * NPAIR], f"w2{_l}": [HIDP, DM],
        f"b2row{_l}": [1, DM], f"lgc{_l}": [128, 8], f"lbc{_l}": [128, 8],
        f"qsrow{_l}": [1, INNER], f"ksrow{_l}": [1, INNER],
    })


def cshapes(cfg):
    d = {
        "identc": [128, 128], "tric": [128, 128], "ones128": [128, 128], "ensel": [16, 16 * 128],
        "onesrow": [1, 512], "negonesrow": [1, 128], "onescol": [128, 1],
        "maskhuge": [128, 128], "cpool": [128, 4],
        "csq": [128, cfg.NT * 32], "snq": [128, cfg.NT * 32],
        "csk": [128, cfg.NT * 32], "snk": [128, cfg.NT * 32],
    }
    for j in (0, 1, 4, 5):
        d[f"m01_{j}"] = [128, 256]
    return d



from contextlib import ExitStack


class KB:
    def __init__(self, cfg):
        self.cfg = cfg
        nc = bass.Bass("TRN2", target_bir_lowering=False,
                       detect_race_conditions=cfg.paranoid)
        self.nc = nc
        self.prog = Prog(nc, paranoid=cfg.paranoid)
        T = cfg.T

        self.io = {}
        self.io["x0"] = nc.dram_tensor("x0", [cfg.NB, DM, T], F32, kind="ExternalInput").ap()
        self.io["out"] = nc.dram_tensor("out", [cfg.NB, DM, T], F32, kind="ExternalOutput").ap()
        for n, sh in {**WSHAPES, **cshapes(cfg)}.items():
            self.io[n] = nc.dram_tensor(n, sh, F32, kind="ExternalInput").ap()
        import os as _osd
        _K = "ExternalOutput" if _osd.environ.get("DBG_SCRATCH") else "Internal"
        if _osd.environ.get("DBG_SCRATCH"):
            self.io["dbg1"] = nc.dram_tensor("dbg1", [128, 16], F32, kind="ExternalOutput").ap()
            self.io["dbg2"] = nc.dram_tensor("dbg2", [128, 512], F32, kind="ExternalOutput").ap()
            self.io["dbg3"] = nc.dram_tensor("dbg3", [128, 512], F32, kind="ExternalOutput").ap()
        self.io["x1"] = nc.dram_tensor("x1s", [DM, T], F32, kind=_K).ap()
        self.io["x2"] = nc.dram_tensor("x2s", [DM, T], F32, kind=_K).ap()
        for v in ("vA", "vB", "vC"):
            self.io[v] = nc.dram_tensor(v + "s", [DM, 1 + T], F32, kind=_K).ap()
        self.io["qfm"] = nc.dram_tensor("qfms", [INNER, T], BF16, kind=_K).ap()
        self.io["kfm"] = nc.dram_tensor("kfms", [INNER, T], BF16, kind=_K).ap()
        self.io["vtk"] = nc.dram_tensor("vtks", [128, cfg.NT * 8 * 65], BF16, kind="Internal").ap()
        self.io["glu"] = nc.dram_tensor("glus", [128, cfg.NCH * NPAIR * 512], BF16, kind="Internal").ap()
        self.io["xs"] = nc.dram_tensor("xss", [128, cfg.NCH * 16 * 512], BF16, kind="Internal").ap()
        self.io["sz"] = nc.dram_tensor("szs", [128, cfg.NCH * 16 * 512], BF16, kind="Internal").ap()
        self.io["xdb"] = nc.dram_tensor("xdbs", [96, T], F32, kind="Internal").ap()
        self.io["cbr"] = nc.dram_tensor("cbrs", [16, T], F32, kind="Internal").ap()
        self.io["cum"] = nc.dram_tensor("cums", [1, T], F32, kind="Internal").ap()

        self.PS1 = nc.alloc_psum_tensor("PSA", [128, 2048], F32)
        self.PS2 = nc.alloc_psum_tensor("PSB", [128, 2048], F32)

        A = nc.alloc_sbuf_tensor
        self.ident = A("c_ident", [128, 128], BF16)
        self.identf = A("c_identf", [128, 128], F32)
        self.tri = A("c_tri", [128, 128], BF16)
        self.ones128 = A("c_ones128", [128, 128], BF16)
        self.ensel = A("c_ensel", [16, 16 * 128], BF16)
        self.onesrow = A("c_onesrow", [1, 512], BF16)
        self.onesrowf = A("c_onesrowf", [1, 512], F32)
        self.negonesrow = A("c_negonesrow", [1, 128], BF16)
        self.onescol = A("c_onescol", [128, 1], BF16)
        self.maskhuge = A("c_maskhuge", [128, 128], F32)
        self.cpool = A("c_cpool", [128, 4], F32)
        nc.const_aps.register(F32, 1e-5, self.cpool[:, 0:1])
        nc.const_aps.register(F32, 1e-20, self.cpool[:, 1:2])
        self.m01 = {j: A(f"c_m01_{j}", [128, 256], BF16) for j in (0, 1, 4, 5)}
        self.convb = A("c_convb", [128, 16], F32)
        self.dcol = A("c_dcol", [128, 16], F32)
        self.acols = A("c_acols", [16, 2], F32)
        self.abc = A("c_abc", [128, 16], F32)
        self.narow = A("c_narow", [1, 16], F32)
        self.rowsc = A("c_rows", [1, 2048], F32)
        self.xst = [A(f"p_xst{i}", [128, 8 * 513], F32) for i in range(2)]
        self.vout = [A(f"p_vout{i}", [128, 512], F32) for i in range(2)]
        self.lnw = A("p_lnw", [128, 512], F32)
        self.zc = A("p_zc", [128, 8], F32)

        self._scopes = []

        p = self.prog
        io = self.io
        castl = [
            (self.ident, "identc"), (self.tri, "tric"), (self.ensel, "ensel"),
            (self.ones128, "ones128"), (self.onesrow, "onesrow"), (self.negonesrow, "negonesrow"),
            (self.onescol, "onescol"),
        ] + [(self.m01[j], f"m01_{j}") for j in (0, 1, 4, 5)]
        for t_, n_ in castl:
            p.dma("gpsimd", t_[:], io[n_], "ldc", writes=[("c", n_)], tag=f"ldc_{n_}")
        plain = [
            (self.identf, "identc"), (self.onesrowf, "onesrow"),
            (self.maskhuge, "maskhuge"), (self.cpool, "cpool"),
            (self.convb, "convbcol"), (self.dcol, "dcol"),
        ]
        for t_, n_ in plain:
            p.dma("sync", t_[:], io[n_], "ldcf", writes=[("c", "f" + n_)], tag=f"ldcf_{n_}")
        p.dma("sync", self.acols[0:16, 1:2], io["alogcol"], "ldcf",
              writes=[("al",)], tag="ld_alog")
        p.op("scalar", lambda e: e.activation(self.acols[0:16, 1:2], self.acols[0:16, 1:2], AF.Exp),
             reads=[("al",)], writes=[("negA",)], tag="negA")
        p.op("vector", lambda e: e.tensor_scalar(self.acols[0:16, 0:1], self.acols[0:16, 1:2], -1.0, None, ALU.mult),
             reads=[("negA",)], writes=[("Acol",)], tag="Acol")
        p.op("tensor", lambda e: e.transpose(self.PS1[0:1, 0:16], self.acols[0:16, 0:1], self.identf[0:16, 0:16]),
             reads=[("Acol",), ("c", "fidentc")], writes=[("ps", 0)], tag="arowT")
        p.op("scalar", lambda e: e.copy(self.rowsc[0:1, 0:16], self.PS1[0:1, 0:16]),
             reads=[("ps", 0)], writes=[("Arowtmp",)], tag="arow")
        p.op("tensor", lambda e: e.matmul(self.PS1[:, 512:528], self.onesrowf[0:1, 0:128],
                                          self.rowsc[0:1, 0:16], start=True, stop=True),
             reads=[("Arowtmp",), ("c", "fonesrow")], writes=[("ps", 1)], tag="abcmm")
        p.op("scalar", lambda e: e.copy(self.abc[:], self.PS1[:, 512:528]),
             reads=[("ps", 1)], writes=[("abc",)], tag="abc")
        self.nabc = A("c_nabc", [128, 16], F32)
        p.op("vector", lambda e: e.tensor_scalar(self.nabc[:], self.abc[:], -1.0, None, ALU.mult),
             reads=[("abc",)], writes=[("nabc",)], tag="nabc")
        p.op("tensor", lambda e: e.transpose(self.PS1[0:1, 0:16], self.acols[0:16, 1:2], self.identf[0:16, 0:16]),
             reads=[("negA",), ("c", "fidentc"), ("Arowtmp",)], writes=[("ps", 0)], tag="negArowT")
        p.op("scalar", lambda e: e.copy(self.narow[0:1, :], self.PS1[0:1, 0:16]),
             reads=[("ps", 0)], writes=[("narow",)], tag="negArow")

    def open_scope(self):
        self._scopes.append(ExitStack())
        return self._scopes[-1]

    def salloc(self, name, shape, dtype):
        st = self._scopes[-1]
        guard = self.nc.sbuf_tensor(f"{name}_{len(self.prog.ops)}", shape, dtype)
        return st.enter_context(guard)

    def close_scope(self):
        self._scopes.pop().close()

    def ps(self, bank, off=0, n=512):
        base = self.PS1 if bank < 4 else self.PS2
        b = bank % 4
        return base[:, b * 512 + off: b * 512 + off + n]

    def psb16(self, bank, off=0, n=512):
        base = self.PS1 if bank < 4 else self.PS2
        b = bank % 4
        return base.bitcast(BF16)[:, b * 1024 + off: b * 1024 + off + n]

    def pk(self, bank):
        return ("ps", bank)

    def ln_chunk(self, L, src_views, okey, hn_dst, hn_key, affine, v0_dst=None,
                 v0_key=None, v0_post=None, banks=(6, 7, 2, 3), tag=""):
        p, ps, pk = self.prog, self.ps, self.pk
        bS1, bS2, bP, bQ = banks
        xb = L.lnxb
        xsq = L.lnxsq
        for k in range(8):
            p.op("vector", lambda e, k=k: e.tensor_copy(xb[:, k * 512:(k + 1) * 512], src_views[k]),
                 reads=okey, writes=[("lnxb", k)], tag=f"{tag}lncp{k}")
            p.op("scalar", lambda e, k=k: e.activation(xsq[:, k * 512:(k + 1) * 512], src_views[k], AF.Square),
                 reads=okey, writes=[("lnxsq", k)], tag=f"{tag}lnsq{k}")

        def mm_s1(e):
            r = None
            for k in range(8):
                r = e.matmul(ps(bS1, 0, 512)[0:1, :], self.onescol[:], xb[:, k * 512:(k + 1) * 512],
                             start=(k == 0), stop=(k == 7))
            return r

        def mm_s2(e):
            r = None
            for k in range(8):
                r = e.matmul(ps(bS2, 0, 512)[0:1, :], self.onescol[:], xsq[:, k * 512:(k + 1) * 512],
                             start=(k == 0), stop=(k == 7))
            return r
        p.op("tensor", mm_s1, reads=[("lnxb", k) for k in range(8)] + [("c", "onescol")],
             writes=[pk(bS1)], tag=f"{tag}lnS1")
        p.op("tensor", mm_s2, reads=[("lnxsq", k) for k in range(8)] + [("c", "onescol")],
             writes=[pk(bS2)], tag=f"{tag}lnS2")
        rows = self.rowsc
        mu = rows[0:1, 0:512]
        var = rows[0:1, 512:1024]
        Pr = rows[0:1, 1024:1536]
        Qr = rows[0:1, 1536:2048]
        p.op("vector", lambda e: e.tensor_scalar(mu, ps(bS1, 0, 512)[0:1, :], 1.0 / DM, None, ALU.mult),
             reads=[pk(bS1)], writes=[("r", "mu")], tag=f"{tag}mu")
        p.op("vector", lambda e: e.tensor_scalar(var, ps(bS2, 0, 512)[0:1, :], 1.0 / DM, None, ALU.mult),
             reads=[pk(bS2)], writes=[("r", "var")], tag=f"{tag}var")
        p.op("scalar", lambda e: e.activation(Pr, mu, AF.Square),
             reads=[("r", "mu")], writes=[("r", "mu2")], tag=f"{tag}mu2")
        p.op("vector", lambda e: e.tensor_tensor(var, var, Pr, ALU.subtract),
             reads=[("r", "var"), ("r", "mu2")], writes=[("r", "var")], tag=f"{tag}var2")
        p.op("scalar", lambda e: e.activation(Pr, var, AF.Ln, bias=1e-5),
             reads=[("r", "var")], writes=[("r", "lnv")], tag=f"{tag}lnv")
        p.op("scalar", lambda e: e.activation(Pr, Pr, AF.Exp, scale=-0.5),
             reads=[("r", "lnv")], writes=[("r", "P")], tag=f"{tag}rstd")
        p.op("vector", lambda e: e.scalar_tensor_tensor(Qr, mu, -1.0, Pr, ALU.mult, ALU.mult),
             reads=[("r", "mu"), ("r", "P")], writes=[("r", "Q")], tag=f"{tag}Q")
        p.op("tensor", lambda e: e.matmul(ps(bP, 0, 512), self.onesrowf[0:1, 0:128], Pr, start=True, stop=True),
             reads=[("r", "P"), ("c", "fonesrow")], writes=[pk(bP)], tag=f"{tag}Pbc")
        p.op("tensor", lambda e: e.matmul(ps(bQ, 0, 512), self.onesrowf[0:1, 0:128], Qr, start=True, stop=True),
             reads=[("r", "Q"), ("c", "fonesrow")], writes=[pk(bQ)], tag=f"{tag}Qbc")
        w = self.lnw[:]
        for k in range(8):
            p.op("vector", lambda e, k=k: e.tensor_tensor(w, src_views[k], ps(bP, 0, 512), ALU.mult),
                 reads=list(okey) + [pk(bP)], writes=[("lnw",)], tag=f"{tag}lnw{k}")
            if affine is None:
                if v0_dst is not None:
                    p.op("vector", lambda e, k=k: e.tensor_tensor(v0_dst(k), w, ps(bQ, 0, 512), ALU.add),
                         reads=[("lnw",), pk(bQ)], writes=[v0_key(k)], tag=f"{tag}v0{k}")
                    p.op("vector", lambda e, k=k: e.tensor_copy(hn_dst(k), v0_dst(k)),
                         reads=[v0_key(k)], writes=[hn_key], tag=f"{tag}hn{k}")
                    v0_post(k)
                else:
                    p.op("vector", lambda e, k=k: e.tensor_tensor(hn_dst(k), w, ps(bQ, 0, 512), ALU.add),
                         reads=[("lnw",), pk(bQ)], writes=[hn_key], tag=f"{tag}hn{k}")
            else:
                lgc, lbc = affine
                p.op("vector", lambda e, k=k: e.tensor_tensor(w, w, ps(bQ, 0, 512), ALU.add),
                     reads=[("lnw",), pk(bQ)], writes=[("lnw",)], tag=f"{tag}lnw2{k}")
                p.op("vector", lambda e, k=k: e.tensor_scalar(
                        hn_dst(k), w, lgc[:, k:k + 1], lbc[:, k:k + 1], ALU.mult, ALU.add),
                     reads=[("lnw",), ("lgc",), ("lbc",)], writes=[hn_key], tag=f"{tag}hn{k}")


def bcast_free(apx, reps):
    import dataclasses
    a = list(apx.ap)
    return dataclasses.replace(apx, ap=type(apx.ap)([a[0], [0, reps], a[1]]))


def bcast_dram(apx, parts):
    import dataclasses
    return dataclasses.replace(apx, ap=type(apx.ap)([[0, parts]] + list(apx.ap)[1:]))



def pass_attn_qkv(kb, tag, lw, xin, guarded, vres_name, first_ln):
    cfg, p, io = kb.cfg, kb.prog, kb.io
    T, CH, NCH = cfg.T, cfg.CH, cfg.NCH
    ps, pk, psb16 = kb.ps, kb.pk, kb.psb16
    tg = tag

    kb.open_scope()
    L = type("L", (), {})()
    L.wqkvb = kb.salloc("wqkvb", [128, 8 * 1536], BF16)
    L.qsf = kb.salloc("qsf", [128, 512], F32)
    L.ksf = kb.salloc("ksf", [128, 512], F32)
    L.hbst = kb.salloc("hbst", [128, 8 * 513], BF16)
    L.lnxb = kb.salloc("lnxb", [128, 8 * 512], BF16)
    L.lnxsq = kb.salloc("lnxsq", [128, 8 * 512], BF16)
    L.qsq = kb.salloc("qsq", [128, 512], F32)
    L.rln = kb.salloc("rln", [128, 16], F32)
    L.uq = kb.salloc("uq", [128, 512], F32)
    L.qn = kb.salloc("qn", [128, 512], F32)
    L.qA = kb.salloc("qA", [128, 512], F32)
    L.qB = kb.salloc("qB", [128, 512], F32)
    L.qrot = kb.salloc("qrot", [128, 512], BF16)
    L.qkstg = {s: kb.salloc(f"qkstg{s}", [128, 4 * 512], BF16) for s in ("q", "k")}
    L.vstg = kb.salloc("vstg", [128, 4 * 8 * 65], BF16)
    L.tbl = {}
    for nm in ("csq", "snq", "csk", "snk"):
        L.tbl[nm] = kb.salloc(nm, [128, cfg.NT * 32], F32)
        p.dma("sync", L.tbl[nm][:], io[nm], "wq2", writes=[("c", "f" + nm)], tag=tg + "ld" + nm)

    p.dma("gpsimd", L.wqkvb[:].rearrange("p (a n) -> p a n", a=8), io[f"wqkv{lw}"].rearrange("(a p) n -> p a n", p=128),
          "wq", writes=[("wqkvb",)], tag=tg + "ldwqkv")
    p.dma("sync", L.qsf[:], bcast_dram(io[f"qsrow{lw}"], 128), "wq2",
          writes=[("qsf",)], tag=tg + "ldqs")
    p.dma("sync", L.ksf[:], bcast_dram(io[f"ksrow{lw}"], 128), "wq2",
          writes=[("ksf",)], tag=tg + "ldks")
    vstg3 = L.vstg[:].rearrange("p (t h x) -> p t h x", h=8, x=65)
    p.op("vector", lambda e: e.memset(vstg3[:, :, :, 64:65], 1.0),
         reads=[], writes=[("vstg",)], tag=tg + "vones")

    hb = L.hbst
    for c in range(NCH):
        if first_ln:
            xs = kb.xst[c % 2]
            p.dma("sync", xs[:, 0:8 * 512].rearrange("p (a n) -> p a n", a=8),
                  xin[:, c * CH:(c + 1) * CH].rearrange("(a p) n -> p a n", p=128),
                  f"xl{c % 2}", writes=[("xst", c % 2)], tag=tg + f"ldx{c}")
            src = [xs[:, k * 512:(k + 1) * 512] for k in range(8)]
            if c == 0:
                p.op("vector", lambda e: e.memset(
                    hb[:].rearrange("p (a n) -> p a n", n=513)[:, :, 0:1], 0.0),
                    reads=[("hb",)], writes=[("hb",)], tag=tg + "hb0")
            else:
                p.op("vector", lambda e: e.tensor_copy(
                    hb[:].rearrange("p (a n) -> p a n", n=513)[:, :, 0:1],
                    hb[:].rearrange("p (a n) -> p a n", n=513)[:, :, 512:513]),
                    reads=[("hb",)], writes=[("hb",)], tag=tg + f"hbt{c}")
            v0c = kb.vout

            def v0_post(k, c=c):
                p.dma("gpsimd", io[vres_name][k * 128:(k + 1) * 128, 1 + c * CH: 1 + (c + 1) * CH],
                      v0c[k % 2][:], f"vst{k % 2}", reads=[("voutbuf", k % 2)],
                      writes=[("v0", c, k)], tag=tg + f"stv0_{c}_{k}")
            kb.ln_chunk(L, src, [("xst", c % 2)],
                        hn_dst=lambda k: hb[:, k * 513 + 1: k * 513 + 513],
                        hn_key=("hb",), affine=None,
                        v0_dst=lambda k: v0c[k % 2][:, 0:512],
                        v0_key=lambda k: ("voutbuf", k % 2),
                        v0_post=v0_post, tag=tg + f"ln{c}_")
        else:
            p.dma("gpsimd", hb[:].rearrange("p (a n) -> p a n", n=513),
                  xin[:, c * CH: c * CH + 513].rearrange("(a p) n -> p a n", p=128),
                  "hbl", reads=[("hb",)], writes=[("hb",)], tag=tg + f"ldhb{c}")

        for tt in range(4):
            gt = c * 4 + tt
            bq, bk2, bv = (0, 1, 2) if tt % 2 == 0 else (4, 5, 6)

            def mmqkv(e, tt=tt, bq=bq, bk2=bk2, bv=bv):
                r = None
                for k in range(8):
                    off = 1 if k < 4 else 0
                    lhsT = hb[:, k * 513 + off + tt * 128: k * 513 + off + tt * 128 + 128]
                    st = (k == 0)
                    sp = (k == 7)
                    e.matmul(ps(bq, 0, 512), lhsT, L.wqkvb[:, k * 1536 + 0: k * 1536 + 512], start=st, stop=sp)
                    e.matmul(ps(bk2, 0, 512), lhsT, L.wqkvb[:, k * 1536 + 512: k * 1536 + 1024], start=st, stop=sp)
                    r = e.matmul(ps(bv, 0, 512), lhsT, L.wqkvb[:, k * 1536 + 1024: k * 1536 + 1536], start=st, stop=sp)
                return r
            p.op("tensor", mmqkv, reads=[("hb",), ("wqkvb",)],
                 writes=[pk(bq), pk(bk2), pk(bv)], tag=tg + f"qkv{c}_{tt}")

            p.op("scalar", lambda e, tt=tt, bv=bv: e.copy(
                    vstg3[:, tt, :, 0:64], ps(bv, 0, 512)[:].rearrange("p (h x) -> p h x", h=8)),
                 reads=[pk(bv)], writes=[("vstg",)], tag=tg + f"vev{c}_{tt}")

            for side, bqq, scl, cs, sn in (
                    ("q", bq, L.qsf, L.tbl["csq"], L.tbl["snq"]),
                    ("k", bk2, L.ksf, L.tbl["csk"], L.tbl["snk"])):
                sclkey = ("qsf",) if side == "q" else ("ksf",)
                p.op("scalar", lambda e, bqq=bqq: e.activation(L.qsq[:], ps(bqq, 0, 512), AF.Square),
                     reads=[pk(bqq)], writes=[("qsq",)], tag=tg + f"{side}sq{gt}")
                def ssred(e):
                    for h in range(8):
                        e.tensor_reduce(L.rln[:, h:h + 1], L.qsq[:, h * 64:(h + 1) * 64],
                                        AX.X, ALU.add)
                    return e.drain()
                p.op("vector", ssred, reads=[("qsq",)], writes=[("rln",)], tag=tg + f"{side}ss{gt}")
                p.op("vector", lambda e: e.tensor_scalar(L.rln[:, 8:16], L.rln[:, 0:8], 1e-20, None, ALU.max),
                     reads=[("rln",)], writes=[("rln",)], tag=tg + f"{side}mx{gt}")
                p.op("scalar", lambda e: e.activation(L.rln[:, 8:16], L.rln[:, 8:16], AF.Ln),
                     reads=[("rln",)], writes=[("rln",)], tag=tg + f"{side}ln{gt}")
                p.op("scalar", lambda e: e.activation(L.rln[:, 8:16], L.rln[:, 8:16], AF.Exp, scale=-0.5),
                     reads=[("rln",)], writes=[("rln",)], tag=tg + f"{side}ex{gt}")
                import os as _osd2
                if _osd2.environ.get("DBG_SCRATCH") and gt == 0 and side == "q" and tg.startswith("b0h0A0"):
                    p.dma("gpsimd", io["dbg1"], L.rln[:], "dbg", reads=[("rln",)],
                          writes=[("dbg",)], tag="dbgrln")
                    p.op("scalar", lambda e, bqq=bqq: e.copy(kb.lnw[:], ps(bqq, 0, 512)),
                         reads=[pk(bqq)], writes=[("dbgq",)], tag="dbgqps_cp")
                    p.dma("gpsimd", io["dbg2"], kb.lnw[:], "dbg", reads=[("dbgq",)],
                          writes=[("dbg",)], tag="dbgqps")
                    p.dma("gpsimd", io["dbg3"], L.qsq[:], "dbg", reads=[("qsq",)],
                          writes=[("dbg",)], tag="dbgqsq")
                p.op("vector", lambda e, bqq=bqq, scl=scl: e.tensor_tensor(L.uq[:], ps(bqq, 0, 512), scl[:], ALU.mult),
                     reads=[pk(bqq), sclkey], writes=[("uq",)], tag=tg + f"{side}u{gt}")
                qksc = QK_SCALE if side == "q" else None
                for h in range(8):
                    p.op("vector", lambda e, h=h, qksc=qksc: e.tensor_scalar(
                            L.qn[:, h * 64:(h + 1) * 64], L.uq[:, h * 64:(h + 1) * 64],
                            L.rln[:, 8 + h:9 + h], qksc,
                            ALU.mult, ALU.mult if qksc else Ellipsis) if qksc else
                            e.tensor_scalar(
                            L.qn[:, h * 64:(h + 1) * 64], L.uq[:, h * 64:(h + 1) * 64],
                            L.rln[:, 8 + h:9 + h], None, ALU.mult),
                         reads=[("uq",), ("rln",)], writes=[("qn",)], tag=tg + f"{side}n{gt}_{h}")
                csv = cs[:, gt * 32:(gt + 1) * 32]
                snv = sn[:, gt * 32:(gt + 1) * 32]
                qn3 = L.qn[:].rearrange("p (g x) -> p g x", x=32)
                cs3 = bcast_free(csv, 16)
                sn3 = bcast_free(snv, 16)
                ck = "csq" if side == "q" else "csk"
                sk2 = "snq" if side == "q" else "snk"
                p.op("vector", lambda e, cs3=cs3, qn3=qn3: e.tensor_tensor(
                        L.qA[:].rearrange("p (g x) -> p g x", x=32), qn3, cs3, ALU.mult),
                     reads=[("qn",), ("c", "f" + ck)], writes=[("qA",)], tag=tg + f"{side}A{gt}")
                p.op("vector", lambda e, sn3=sn3, qn3=qn3: e.tensor_tensor(
                        L.qB[:].rearrange("p (g x) -> p g x", x=32), qn3, sn3, ALU.mult),
                     reads=[("qn",), ("c", "f" + sk2)], writes=[("qB",)], tag=tg + f"{side}B{gt}")
                qa3 = L.qA[:].rearrange("p (h v x) -> p h v x", v=2, x=32)
                qb3 = L.qB[:].rearrange("p (h v x) -> p h v x", v=2, x=32)
                qr3 = L.qrot[:].rearrange("p (h v x) -> p h v x", v=2, x=32)
                p.op("vector", lambda e, qa3=qa3, qb3=qb3, qr3=qr3: e.tensor_tensor(
                        qr3[:, :, 0, :], qa3[:, :, 0, :], qb3[:, :, 1, :], ALU.subtract),
                     reads=[("qA",), ("qB",)], writes=[("qrot",)], tag=tg + f"{side}r1{gt}")
                p.op("vector", lambda e, qa3=qa3, qb3=qb3, qr3=qr3: e.tensor_tensor(
                        qr3[:, :, 1, :], qa3[:, :, 1, :], qb3[:, :, 0, :], ALU.add),
                     reads=[("qA",), ("qB",)], writes=[("qrot",)], tag=tg + f"{side}r2{gt}")

                def trans(e):
                    r = None
                    for j in range(4):
                        r = e.transpose(psb16(3, j * 256, 128), L.qrot[:, j * 128:(j + 1) * 128], kb.ident[:])
                    return r
                p.op("tensor", trans, reads=[("qrot",), ("c", "identc")],
                     writes=[pk(3)], tag=tg + f"{side}tr{gt}")
                for j in range(4):
                    p.op("scalar", lambda e, side=side, tt=tt, j=j: e.copy(
                            L.qkstg[side][:, j * 512 + tt * 128: j * 512 + tt * 128 + 128],
                            psb16(3, j * 256, 128)),
                         reads=[pk(3)], writes=[("qkstg", side)], tag=tg + f"{side}te{gt}_{j}")

        dstq = io["qfm"]
        dstk = io["kfm"]
        for j in range(4):
            p.dma("gpsimd", dstq[j * 128:(j + 1) * 128, c * CH:(c + 1) * CH],
                  L.qkstg["q"][:, j * 512:(j + 1) * 512], "qst",
                  reads=[("qkstg", "q")], writes=[("qfm", c, j)], tag=tg + f"stq{c}_{j}")
            p.dma("gpsimd", dstk[j * 128:(j + 1) * 128, c * CH:(c + 1) * CH],
                  L.qkstg["k"][:, j * 512:(j + 1) * 512], "kst",
                  reads=[("qkstg", "k")], writes=[("kfm", c, j)], tag=tg + f"stk{c}_{j}")
        p.dma("gpsimd", io["vtk"][:, c * 4 * 520:(c + 1) * 4 * 520],
              L.vstg[:], "vst2", reads=[("vstg",)], writes=[("vtk", c)], tag=tg + f"stv{c}")
    kb.close_scope()



def pass_attn_core(kb, tag, lw, vres_name, vres_guarded, vdst):
    cfg, p, io = kb.cfg, kb.prog, kb.io
    T, CH, NCH, NT = cfg.T, cfg.CH, cfg.NCH, cfg.NT
    ps, pk, psb16 = kb.ps, kb.pk, kb.psb16
    tg = tag

    kb.open_scope()
    L = type("L", (), {})()
    L.vtok = kb.salloc("vtok", [128, NT * 8 * 65], BF16)
    L.qh = [kb.salloc(f"qh{i}", [64, T], BF16) for i in range(2)]
    L.kh = [kb.salloc(f"kh{i}", [64, T], BF16) for i in range(2)]
    L.attnT = kb.salloc("attnT", [128, 2 * 1536], BF16)
    L.mcol = kb.salloc("mcol", [128, NT], F32)
    L.msc = kb.salloc("msc", [NT, 128], F32)
    L.mrow = kb.salloc("mrow", [1, T], BF16)
    L.invS = kb.salloc("invS", [128, 2], F32)
    L.otok = kb.salloc("otok", [128, NT * 512], BF16)
    L.ofm = kb.salloc("ofm", [128, 4 * 512], BF16)
    L.wob = kb.salloc("wob", [128, 4 * DM], BF16)

    p.dma("gpsimd", L.wob[:].rearrange("p (a n) -> p a n", a=4), io[f"wo{lw}"].rearrange("(a p) n -> p a n", p=128),
          "wo", writes=[("wob",)], tag=tg + "ldwo")
    p.dma("sync", L.vtok[:], io["vtk"], "vtl", writes=[("vtok",)], tag=tg + "ldvt")
    vtok3 = L.vtok[:].rearrange("p (t h x) -> p t h x", h=8, x=65)

    for h in range(HEADS):
        par = h % 2
        p.dma("sync", L.qh[par][:], io["qfm"][h * 64:(h + 1) * 64, :], f"qh{par}",
              reads=[("qh", par)], writes=[("qh", par)], tag=tg + f"ldq{h}")
        p.dma("sync", L.kh[par][:], io["kfm"][h * 64:(h + 1) * 64, :], f"kh{par}",
              reads=[("kh", par)], writes=[("kh", par)], tag=tg + f"ldk{h}")
        qv, kv = L.qh[par], L.kh[par]
        for qt in range(NT):
            start = max(0, qt * 128 - 512)
            n = qt * 128 + 128 - start
            bk0 = 0 if qt % 2 == 0 else 2

            def mmsim(e, qt=qt, start=start, n=n, bk0=bk0, qv=qv, kv=kv):
                lhsT = qv[:, qt * 128: qt * 128 + 128]
                r = None
                o = 0
                while o < n:
                    wdt = min(512, n - o)
                    r = e.matmul(ps(bk0, 0, 1024)[:, o:o + wdt], lhsT,
                                 kv[:, start + o: start + o + wdt],
                                 start=True, stop=True)
                    o += wdt
                return r
            p.op("tensor", mmsim, reads=[("qh", par), ("kh", par)],
                 writes=[pk(bk0), pk(bk0 + 1)], tag=tg + f"sim{h}_{qt}")
            p.op("vector", lambda e, qt=qt, n=n, bk0=bk0: e.tensor_reduce(
                    L.mcol[:, qt:qt + 1], ps(bk0, 0, 1024)[:, 0:n], AX.X, ALU.max),
                 reads=[pk(bk0), pk(bk0 + 1)], writes=[("mcol",)], tag=tg + f"max{h}_{qt}")
        p.op("tensor", lambda e: e.transpose(ps(4, 0, 128)[0:NT, :], L.mcol[:, 0:NT], kb.identf[:]),
             reads=[("mcol",), ("c", "fidentc")], writes=[pk(4)], tag=tg + f"mtr{h}")
        p.op("scalar", lambda e: e.copy(L.msc[0:NT, :], ps(4, 0, 128)[0:NT, :]),
             reads=[pk(4)], writes=[("msc",)], tag=tg + f"mte{h}")
        p.dma("gpsimd", L.mrow[0:1, :].rearrange("a (t p) -> a t p", p=128),
              L.msc[0:NT, :], "mrow", reads=[("msc",), ("mrow",)], writes=[("mrow",)],
              tag=tg + f"mrow{h}")

        for qc in range(T // 256):
            j0 = max(0, 4 - 2 * qc)
            nj = 6 - j0
            bk0 = 0 if qc % 2 == 0 else 4

            def mmsim2(e, qc=qc, j0=j0, bk0=bk0, qv=qv, kv=kv):
                r = None
                for j in range(j0, 6):
                    kstart = qc * 256 - 512 + j * 128
                    o = (j - j0) * 256
                    e.matmul(ps(bk0, 0, 2048)[:, o:o + 256],
                             kv[:, kstart: kstart + 128],
                             qv[:, qc * 256: qc * 256 + 256],
                             start=True, stop=False)
                    r = e.matmul(ps(bk0, 0, 2048)[:, o:o + 256],
                                 kb.negonesrow[0:1, :],
                                 L.mrow[0:1, qc * 256: qc * 256 + 256],
                                 start=False, stop=True)
                return r
            pks2 = [pk(bk0), pk(bk0 + 1), pk(bk0 + 2)]
            ato = 1536 * (qc % 2)
            p.op("tensor", mmsim2, reads=[("qh", par), ("kh", par), ("mrow",), ("c", "negonesrow")],
                 writes=pks2, tag=tg + f"sim2_{h}_{qc}")
            p.op("scalar", lambda e, nj=nj, bk0=bk0, ato=ato: e.activation(
                    L.attnT[:, ato:ato + nj * 256], ps(bk0, 0, 2048)[:, 0:nj * 256], AF.Exp),
                 reads=pks2, writes=[("attnT", qc % 2)], tag=tg + f"exp{h}_{qc}")
            for j in (0, 1, 4, 5):
                if j < j0:
                    continue
                o = (j - j0) * 256
                p.op("vector", lambda e, o=o, j=j, ato=ato: e.tensor_tensor(
                        L.attnT[:, ato + o:ato + o + 256], L.attnT[:, ato + o:ato + o + 256],
                        kb.m01[j][:], ALU.mult),
                     reads=[("attnT", qc % 2), ("c", f"m01_{j}")], writes=[("attnT", qc % 2)],
                     tag=tg + f"msk{h}_{qc}_{j}")
            for half in range(2):
                bav = 3 if half == 0 else 7

                def mmav(e, qc=qc, j0=j0, half=half, h=h, bav=bav, ato=ato):
                    r = None
                    for j in range(j0, 6):
                        ktg = qc * 2 - 4 + j
                        r = e.matmul(ps(bav, 0, 65),
                                     L.attnT[:, ato + (j - j0) * 256 + half * 128: ato + (j - j0) * 256 + half * 128 + 128],
                                     vtok3[:, ktg, h, :],
                                     start=(j == j0), stop=(j == 5))
                    return r
                p.op("tensor", mmav, reads=[("attnT", qc % 2), ("vtok",)],
                     writes=[pk(bav)], tag=tg + f"av{h}_{qc}_{half}")
                p.op("vector", lambda e, bav=bav, half=half: e.reciprocal(
                        L.invS[:, half:half + 1], ps(bav, 64, 1)),
                     reads=[pk(bav)], writes=[("invS",)], tag=tg + f"is{h}_{qc}_{half}")
                p.op("scalar", lambda e, qc=qc, half=half, h=h, bav=bav: e.activation(
                        L.otok[:, (qc * 2 + half) * 512 + h * 64: (qc * 2 + half) * 512 + h * 64 + 64],
                        ps(bav, 0, 64), AF.Copy, scale=L.invS[:, half:half + 1]),
                     reads=[pk(bav), ("invS",)], writes=[("otok",)], tag=tg + f"oev{h}_{qc}_{half}")

    for c in range(NCH):
        xs = kb.xst[c % 2]
        p.dma("sync", xs[:, 0:8 * 512].rearrange("p (a n) -> p a n", a=8),
              io[vres_name][:, (1 if vres_guarded else 0) + c * CH:
                            (1 if vres_guarded else 0) + (c + 1) * CH]
              .rearrange("(a p) n -> p a n", p=128),
              f"xl{c % 2}", writes=[("xst", c % 2)], tag=tg + f"ldvr{c}")
        for tt in range(4):
            gt = c * 4 + tt

            def otr(e, gt=gt):
                r = None
                for j in range(4):
                    r = e.transpose(psb16(3, j * 256, 128),
                                    L.otok[:, gt * 512 + j * 128: gt * 512 + j * 128 + 128],
                                    kb.ident[:])
                return r
            p.op("tensor", otr, reads=[("otok",), ("c", "identc")], writes=[pk(3)],
                 tag=tg + f"otr{gt}")
            for j in range(4):
                p.op("scalar", lambda e, j=j, tt=tt: e.copy(
                        L.ofm[:, j * 512 + tt * 128: j * 512 + tt * 128 + 128], psb16(3, j * 256, 128)),
                     reads=[pk(3)], writes=[("ofm",)], tag=tg + f"ote{gt}_{j}")
        for m in range(8):
            bw = m % 2

            def mmwo(e, m=m, bw=bw):
                r = None
                for k in range(4):
                    r = e.matmul(ps(bw, 0, 512), L.wob[:, k * DM + m * 128: k * DM + m * 128 + 128],
                                 L.ofm[:, k * 512:(k + 1) * 512], start=(k == 0), stop=(k == 3))
                return r
            p.op("tensor", mmwo, reads=[("ofm",), ("wob",)], writes=[pk(bw)], tag=tg + f"wo{c}_{m}")
            p.op("vector", lambda e, m=m, bw=bw, xs=xs: e.tensor_tensor(
                    kb.vout[m % 2][:], ps(bw, 0, 512), xs[:, m * 512:(m + 1) * 512], ALU.add),
                 reads=[pk(bw), ("xst", c % 2)], writes=[("voutbuf", m % 2)], tag=tg + f"vo{c}_{m}")
            p.dma("gpsimd", vdst[:, 1 + c * CH: 1 + (c + 1) * CH]
                  .rearrange("(a p) n -> p a n", p=128)[:, m, :],
                  kb.vout[m % 2][:], f"vst{m % 2}", reads=[("voutbuf", m % 2)],
                  writes=[("vdst", c, m)], tag=tg + f"stv{c}_{m}")
    kb.close_scope()

def pass_mlp1(kb, tag, lw, vsrc_name):
    cfg, p, io = kb.cfg, kb.prog, kb.io
    T, CH, NCH = cfg.T, cfg.CH, cfg.NCH
    ps, pk = kb.ps, kb.pk
    tg = tag

    kb.open_scope()
    L = type("L", (), {})()
    L.w1b = kb.salloc("w1b", [128, 8 * W1P], BF16)
    L.b1c = kb.salloc("b1c", [128, 2 * NPAIR], F32)
    L.lgc = kb.salloc("lgc", [128, 8], F32)
    L.lbc = kb.salloc("lbc", [128, 8], F32)
    L.hnst = kb.salloc("hnst", [128, 8 * 512], BF16)
    L.glustg = kb.salloc("glustg", [128, 2 * 512], BF16)
    L.gelu_sb = kb.salloc("gelu", [128, 512], BF16)
    L.lnxb = kb.salloc("lnxb", [128, 8 * 512], BF16)
    L.lnxsq = kb.salloc("lnxsq", [128, 8 * 512], BF16)

    p.dma("gpsimd", L.w1b[:].rearrange("p (a n) -> p a n", a=8), io[f"w1{lw}"].rearrange("(a p) n -> p a n", p=128),
          "w1", writes=[("w1b",)], tag=tg + "ldw1")
    p.dma("sync", L.b1c[:], io[f"b1c{lw}"], "wmr", writes=[("b1c",)], tag=tg + "ldb1")
    p.dma("sync", L.lgc[:], io[f"lgc{lw}"], "wmr", writes=[("lgc",)], tag=tg + "ldlg")
    p.dma("sync", L.lbc[:], io[f"lbc{lw}"], "wmr", writes=[("lbc",)], tag=tg + "ldlb")

    for c in range(NCH):
        xs = kb.xst[c % 2]
        p.dma("sync", xs[:].rearrange("p (a n) -> p a n", n=513),
              io[vsrc_name][:, c * CH: c * CH + 513].rearrange("(a p) n -> p a n", p=128),
              f"xl{c % 2}", writes=[("xst", c % 2)], tag=tg + f"ldv{c}")
        sh = [xs[:, k * 513 + (1 if k < 4 else 0): k * 513 + (1 if k < 4 else 0) + 512]
              for k in range(8)]
        kb.ln_chunk(L, sh, [("xst", c % 2)],
                    hn_dst=lambda k: L.hnst[:, k * 512:(k + 1) * 512],
                    hn_key=("hnst",), affine=(L.lgc, L.lbc), tag=tg + f"ln{c}_")
        for t in range(NPAIR):
            ba, bg = (0, 1) if t % 2 == 0 else (4, 5)

            def mmw1(e, t=t, ba=ba, bg=bg):
                r = None
                for k in range(8):
                    rhs = L.hnst[:, k * 512:(k + 1) * 512]
                    e.matmul(ps(ba, 0, 512), L.w1b[:, k * W1P + (2 * t) * 128: k * W1P + (2 * t) * 128 + 128],
                             rhs, start=(k == 0), stop=(k == 7))
                    r = e.matmul(ps(bg, 0, 512), L.w1b[:, k * W1P + (2 * t + 1) * 128: k * W1P + (2 * t + 1) * 128 + 128],
                                 rhs, start=(k == 0), stop=(k == 7))
                return r
            p.op("tensor", mmw1, reads=[("hnst",), ("w1b",)], writes=[pk(ba), pk(bg)],
                 tag=tg + f"w1_{c}_{t}")
            p.op("scalar", lambda e, t=t, bg=bg: e.activation(
                    L.gelu_sb[:], ps(bg, 0, 512), AF.Gelu, bias=L.b1c[:, 2 * t + 1: 2 * t + 2]),
                 reads=[pk(bg), ("b1c",)], writes=[("gelu",)], tag=tg + f"gel{c}_{t}")
            p.op("vector", lambda e, t=t, ba=ba: e.scalar_tensor_tensor(
                    L.glustg[:, (t % 2) * 512: (t % 2) * 512 + 512], ps(ba, 0, 512),
                    L.b1c[:, 2 * t: 2 * t + 1], L.gelu_sb[:], ALU.add, ALU.mult),
                 reads=[pk(ba), ("gelu",), ("b1c",)], writes=[("glustg", t % 2)],
                 tag=tg + f"glu{c}_{t}")
            p.dma("gpsimd", io["glu"][:, (c * NPAIR + t) * 512: (c * NPAIR + t + 1) * 512],
                  L.glustg[:, (t % 2) * 512: (t % 2) * 512 + 512], f"gst{t % 2}",
                  reads=[("glustg", t % 2)], writes=[("glu", c, t)], tag=tg + f"stg{c}_{t}")
    kb.close_scope()



def pass_mlp2(kb, tag, lw, vsrc_name, vdst, final, xres, xres_guarded):
    cfg, p, io = kb.cfg, kb.prog, kb.io
    T, CH, NCH = cfg.T, cfg.CH, cfg.NCH
    ps, pk = kb.ps, kb.pk
    tg = tag

    kb.open_scope()
    L = type("L", (), {})()
    L.w2b = kb.salloc("w2b", [128, NPAIR * DM], BF16)
    L.b2r = kb.salloc("b2r", [1, DM], BF16)
    L.glu = kb.salloc("glu", [128, NPAIR * 512], BF16)
    L.x2st = kb.salloc("x2st", [128, 8 * 512], F32) if final else None

    p.dma("gpsimd", L.w2b[:].rearrange("p (a n) -> p a n", a=NPAIR), io[f"w2{lw}"].rearrange("(a p) n -> p a n", p=128),
          "w2", writes=[("w2b",)], tag=tg + "ldw2")
    p.dma("gpsimd", L.b2r[:], io[f"b2row{lw}"], "wmr2", writes=[("b2r",)], tag=tg + "ldb2")

    for c in range(NCH):
        xs = kb.xst[c % 2]
        p.dma("sync", xs[:].rearrange("p (a n) -> p a n", n=513),
              io[vsrc_name][:, c * CH: c * CH + 513].rearrange("(a p) n -> p a n", p=128),
              f"xl{c % 2}", writes=[("xst", c % 2)], tag=tg + f"ldv{c}")
        raw = [xs[:, k * 513 + 1: k * 513 + 513] for k in range(8)]
        p.dma("sync", L.glu[:], io["glu"][:, c * NPAIR * 512: (c + 1) * NPAIR * 512],
              "glul", reads=[("glub",)], writes=[("glub",)], tag=tg + f"ldg{c}")
        if final:
            p.dma("sync", L.x2st[:].rearrange("p (a n) -> p a n", n=512),
                  (xres[:, 1 + c * CH: 1 + (c + 1) * CH] if xres_guarded
                   else xres[:, c * CH: (c + 1) * CH]).rearrange("(a p) n -> p a n", p=128),
                  f"x2l{c % 2}", writes=[("x2st",)], tag=tg + f"ldx{c}")
        for m in range(8):
            bw = m % 2

            def mmw2(e, m=m, bw=bw):
                for k in range(NPAIR):
                    e.matmul(ps(bw, 0, 512), L.w2b[:, k * DM + m * 128: k * DM + m * 128 + 128],
                             L.glu[:, k * 512:(k + 1) * 512], start=(k == 0), stop=False)
                return e.matmul(ps(bw, 0, 512), L.b2r[0:1, m * 128: m * 128 + 128],
                                kb.onesrow[0:1, :], start=False, stop=True)
            p.op("tensor", mmw2, reads=[("glub",), ("w2b",), ("b2r",), ("c", "onesrow")],
                 writes=[pk(bw)], tag=tg + f"w2_{c}_{m}")
            if not final:
                p.op("vector", lambda e, m=m, bw=bw, raw=raw: e.tensor_tensor(
                        kb.vout[m % 2][:], ps(bw, 0, 512), raw[m], ALU.add),
                     reads=[pk(bw), ("xst", c % 2)], writes=[("voutbuf", m % 2)], tag=tg + f"vo{c}_{m}")
            else:
                p.op("vector", lambda e, m=m, raw=raw: e.scalar_tensor_tensor(
                        kb.lnw[:], raw[m], 0.5, L.x2st[:, m * 512:(m + 1) * 512], ALU.mult, ALU.add),
                     reads=[("xst", c % 2), ("x2st",)], writes=[("lnw",)], tag=tg + f"pre{c}_{m}")
                p.op("vector", lambda e, m=m, bw=bw: e.scalar_tensor_tensor(
                        kb.vout[m % 2][:], ps(bw, 0, 512), 0.5, kb.lnw[:], ALU.mult, ALU.add),
                     reads=[pk(bw), ("lnw",)], writes=[("voutbuf", m % 2)], tag=tg + f"xo{c}_{m}")
            col0 = (c * CH) if final else (1 + c * CH)
            p.dma("gpsimd", vdst[:, col0: col0 + CH].rearrange("(a p) n -> p a n", p=128)[:, m, :],
                  kb.vout[m % 2][:], f"vst{m % 2}", reads=[("voutbuf", m % 2)],
                  writes=[("vdst", c, m)], tag=tg + f"stv{c}_{m}")
    kb.close_scope()



def pass_mamba1(kb, tag, xin):
    cfg, p, io = kb.cfg, kb.prog, kb.io
    T, CH, NCH = cfg.T, cfg.CH, cfg.NCH
    ps, pk = kb.ps, kb.pk
    tg = tag

    kb.open_scope()
    L = type("L", (), {})()
    L.winb = kb.salloc("winb", [128, 8 * 2 * DI], BF16)
    L.cdiag = kb.salloc("cdiag", [128, DC * 16 * 128], BF16)
    L.wxb = kb.salloc("wxb", [128, 16 * 96], BF16)
    L.xcs = kb.salloc("xcs", [128, 16 * 515], BF16)
    L.hnst = kb.salloc("hnst", [128, 8 * 512], BF16)
    L.lnxb = kb.salloc("lnxb", [128, 8 * 512], BF16)
    L.lnxsq = kb.salloc("lnxsq", [128, 8 * 512], BF16)
    L.xsc = kb.salloc("xsc", [128, 16 * 512], BF16)
    L.szstg = kb.salloc("szstg", [128, 2 * 512], BF16)
    L.xdbst = kb.salloc("xdbst", [96, 512], F32)

    p.dma("gpsimd", L.winb[:].rearrange("p (a n) -> p a n", a=8), io["win"].rearrange("(a p) n -> p a n", p=128),
          "wmi", writes=[("winb",)], tag=tg + "ldwin")
    p.dma("gpsimd", L.cdiag[:].rearrange("p (a n) -> p a n", a=DC * 16), io["convdiag"].rearrange("(a p) n -> p a n", p=128),
          "wmc", writes=[("cdiag",)], tag=tg + "ldcd")
    p.dma("gpsimd", L.wxb[:].rearrange("p (a n) -> p a n", a=16), io["wx96"].rearrange("(a p) n -> p a n", p=128),
          "wmx", writes=[("wxb",)], tag=tg + "ldwx")

    xcs3 = L.xcs[:].rearrange("p (a n) -> p a n", n=515)
    for c in range(NCH):
        xs = kb.xst[c % 2]
        p.dma("sync", xs[:, 0:8 * 512].rearrange("p (a n) -> p a n", a=8),
              xin[:, c * CH:(c + 1) * CH].rearrange("(a p) n -> p a n", p=128),
              f"xl{c % 2}", writes=[("xst", c % 2)], tag=tg + f"ldx{c}")
        src = [xs[:, k * 512:(k + 1) * 512] for k in range(8)]
        kb.ln_chunk(L, src, [("xst", c % 2)],
                    hn_dst=lambda k: L.hnst[:, k * 512:(k + 1) * 512],
                    hn_key=("hnst",), affine=None, tag=tg + f"ln{c}_")
        if c == 0:
            p.op("vector", lambda e: e.memset(xcs3[:, :, 0:3], 0.0),
                 reads=[("xcs",)], writes=[("xcs",)], tag=tg + "xct0")
        else:
            p.op("vector", lambda e: e.tensor_copy(xcs3[:, :, 0:3], xcs3[:, :, 512:515]),
                 reads=[("xcs",)], writes=[("xcs",)], tag=tg + f"xct{c}")
        for m in range(32):
            bw = m % 2

            def mmwin(e, m=m, bw=bw):
                r = None
                for k in range(8):
                    r = e.matmul(ps(bw, 0, 512), L.winb[:, k * 2 * DI + m * 128: k * 2 * DI + m * 128 + 128],
                                 L.hnst[:, k * 512:(k + 1) * 512], start=(k == 0), stop=(k == 7))
                return r
            p.op("tensor", mmwin, reads=[("hnst",), ("winb",)], writes=[pk(bw)], tag=tg + f"wi{c}_{m}")
            if m < 16:
                p.op("scalar", lambda e, m=m, bw=bw: e.copy(xcs3[:, m, 3:515], ps(bw, 0, 512)),
                     reads=[pk(bw)], writes=[("xcs",)], tag=tg + f"xce{c}_{m}")
            else:
                mz = m - 16
                p.op("scalar", lambda e, mz=mz, bw=bw: e.activation(
                        L.szstg[:, (mz % 2) * 512: (mz % 2) * 512 + 512], ps(bw, 0, 512), AF.Silu),
                     reads=[pk(bw)], writes=[("szstg", mz % 2)], tag=tg + f"sze{c}_{mz}")
                p.dma("gpsimd", io["sz"][:, (c * 16 + mz) * 512: (c * 16 + mz + 1) * 512],
                      L.szstg[:, (mz % 2) * 512: (mz % 2) * 512 + 512], f"szst{mz % 2}",
                      reads=[("szstg", mz % 2)], writes=[("sz", c, mz)], tag=tg + f"szs{c}_{mz}")
        for ft in range(16):
            bc = 2 + ft % 2

            def mmcv(e, ft=ft, bc=bc):
                r = None
                for k in range(DC):
                    r = e.matmul(ps(bc, 0, 512), L.cdiag[:, (k * 16 + ft) * 128: (k * 16 + ft) * 128 + 128],
                                 xcs3[:, ft, k: k + 512], start=(k == 0), stop=(k == DC - 1))
                return r
            p.op("tensor", mmcv, reads=[("xcs",), ("cdiag",)], writes=[pk(bc)], tag=tg + f"cv{c}_{ft}")
            p.op("scalar", lambda e, ft=ft, bc=bc: e.activation(
                    L.xsc[:, ft * 512:(ft + 1) * 512], ps(bc, 0, 512), AF.Silu,
                    bias=kb.convb[:, ft: ft + 1]),
                 reads=[pk(bc), ("c", "fconvbcol")], writes=[("xsc", ft)], tag=tg + f"cvs{c}_{ft}")
            p.dma("gpsimd", io["xs"][:, (c * 16 + ft) * 512: (c * 16 + ft + 1) * 512],
                  L.xsc[:, ft * 512:(ft + 1) * 512], "xsst",
                  reads=[("xsc", ft)], writes=[("xs", c, ft)], tag=tg + f"xss{c}_{ft}")

        def mmwx(e):
            r = None
            for k in range(16):
                r = e.matmul(ps(4, 0, 512)[0:96, :], L.wxb[:, k * 96: k * 96 + 96],
                             L.xsc[:, k * 512:(k + 1) * 512], start=(k == 0), stop=(k == 15))
            return r
        p.op("tensor", mmwx, reads=[("xsc", k) for k in range(16)] + [("wxb",)],
             writes=[pk(4)], tag=tg + f"wx{c}")
        p.op("scalar", lambda e: e.copy(L.xdbst[:], ps(4, 0, 512)[0:96, :]),
             reads=[pk(4)], writes=[("xdbst",)], tag=tg + f"xdbe{c}")
        p.dma("gpsimd", io["xdb"][:, c * CH:(c + 1) * CH], L.xdbst[:], "xdst",
              reads=[("xdbst",)], writes=[("xdb", c)], tag=tg + f"xdbs{c}")
        p.dma("gpsimd", io["cbr"][:, c * CH:(c + 1) * CH], L.xdbst[64:80, :], "cbst",
              reads=[("xdbst",)], writes=[("cbr", c)], tag=tg + f"cbrs{c}")
    kb.close_scope()



def pass_mamba2(kb, tag, xin, xout, parts=127):
    cfg, p, io = kb.cfg, kb.prog, kb.io
    T, CH, NCH = cfg.T, cfg.CH, cfg.NCH
    ps, pk, psb16 = kb.ps, kb.pk, kb.psb16
    tg = tag

    kb.open_scope()
    L = type("L", (), {})()
    L.woutb = kb.salloc("woutb", [128, 16 * DM], BF16)
    L.xss = kb.salloc("xss", [128, 16 * 512], BF16)
    L.szs = kb.salloc("szs", [128, 16 * 512], BF16)
    L.ys = kb.salloc("ys", [128, 16 * 512], BF16)
    L.xstt = kb.salloc("xstt", [128, 16 * 512], BF16)
    L.dtbuf = kb.salloc("dtbuf", [128, 16 * 128], BF16)
    L.xdb = kb.salloc("xdb", [96, 512], F32)
    L.cbb = kb.salloc("cbb", [128, 16 * 512], BF16)
    L.ctbb = kb.salloc("ctbb", [128, 512], F32)
    L.cbs = kb.salloc("cbs", [16, 512], BF16)
    L.dts = kb.salloc("dts", [128, 8], F32)
    L.bts = kb.salloc("bts", [128, 4 * 16], F32)
    L.btst = kb.salloc("btst", [128, 4 * 16], F32)
    L.cums = kb.salloc("cums", [128, 4], F32)
    L.cumflat = kb.salloc("cumflat", [1, 512], F32)
    L.cumr4 = kb.salloc("cumr4", [4, 128], F32)
    L.ets = kb.salloc("ets", [16, 512], BF16)
    L.chat = kb.salloc("chat", [16, 512], BF16)
    L.decs = kb.salloc("decs", [16, 512], BF16)
    L.dect = kb.salloc("dect", [128, 4 * 16], BF16)
    L.bhat = kb.salloc("bhat", [128, 4 * 16], BF16)
    L.psb = kb.salloc("psb", [128, 128], F32)
    L.ptmp = kb.salloc("ptmp", [128, 128], BF16)
    L.pb4 = kb.salloc("pb4", [128, 4 * 128], BF16)
    L.hts = kb.salloc("hts", [16, DI], F32)
    L.htb = kb.salloc("htb", [16, DI], BF16)
    L.biass = kb.salloc("biass", [128, 16], F32)
    L.ctm = kb.salloc("ctm", [128, 128], F32)
    L.bqc = kb.salloc("bqc", [16, 2], F32)
    L.db16 = kb.salloc("db16", [128, 4], BF16)
    L.yw = kb.salloc("yw", [128, 512], F32)
    L.drelu = kb.salloc("drelu", [128, 4], F32)

    p.dma("gpsimd", L.woutb[:].rearrange("p (a n) -> p a n", a=16), io["wout"].rearrange("(a p) n -> p a n", p=128),
          "wmo", writes=[("woutb",)], tag=tg + "ldwout")
    p.op("vector", lambda e: e.memset(L.hts[:], 0.0), reads=[], writes=[("hts",)], tag=tg + "h0")
    p.op("vector", lambda e: e.memset(L.htb[:], 0.0), reads=[], writes=[("htb",)], tag=tg + "hb0")

    for c in range(NCH):
        xs = kb.xst[c % 2]
        p.dma("sync", xs[:, 0:8 * 512].rearrange("p (a n) -> p a n", a=8),
              xin[:, c * CH:(c + 1) * CH].rearrange("(a p) n -> p a n", p=128),
              f"xl{c % 2}", writes=[("xst", c % 2)], tag=tg + f"ldx{c}")
        p.dma("sync", L.xss[:], io["xs"][:, c * 16 * 512:(c + 1) * 16 * 512],
              "xsl", reads=[("xss",)], writes=[("xss",)], tag=tg + f"ldxs{c}")
        p.dma("sync", L.szs[:], io["sz"][:, c * 16 * 512:(c + 1) * 16 * 512],
              "szl", reads=[("szs",)], writes=[("szs",)], tag=tg + f"ldsz{c}")
        p.dma("sync", L.xdb[:], io["xdb"][:, c * CH:(c + 1) * CH],
              "xdl", reads=[("xdb",)], writes=[("xdb",)], tag=tg + f"ldxdb{c}")
        import dataclasses as _dc
        import os as _os4
        if int(_os4.environ.get("BCAST_DMA", "1")):
            _cbsrc = io["cbr"][:, c * CH:(c + 1) * CH]
            _cbsrc = _dc.replace(_cbsrc, ap=type(_cbsrc.ap)([[0, 128]] + list(_cbsrc.ap)))
            p.dma("gpsimd", L.cbb[:].rearrange("p (n t) -> p n t", n=16), _cbsrc,
                  "cbl", reads=[("cbb",)], writes=[("cbb",)], tag=tg + f"ldcbb{c}")

        def trdb(e):
            r = None
            for sc in range(4):
                e.transpose(ps(7, 0 + sc * 16, 16), L.xdb[0:16, sc * 128:(sc + 1) * 128],
                            kb.identf[0:16, 0:16])
                r = e.transpose(ps(6, 264 + sc * 16, 16),
                                L.xdb[32:48, sc * 128:(sc + 1) * 128], kb.identf[32:48, 32:48])
            return r
        p.op("tensor", trdb, reads=[("xdb",), ("c", "fidentc")], writes=[pk(6), pk(7)], tag=tg + f"trdb{c}")
        def dtev(e):
            r = None
            for sc in range(4):
                r = e.copy(L.dts[:, sc:sc + 1], ps(7, 0 + sc * 16, 1))
            return r
        p.op("scalar", dtev, reads=[pk(7)], writes=[("dts",)], tag=tg + f"dte{c}")
        p.op("scalar", lambda e: e.copy(L.bts[:], ps(6, 264, 64)),
             reads=[pk(6)], writes=[("bts",)], tag=tg + f"bte{c}")
        p.op("scalar", lambda e: e.activation(L.dts[:, 4:8], L.dts[:, 0:4], AF.Abs),
             reads=[("dts",)], writes=[("dsp",)], tag=tg + f"sp1{c}")
        p.op("scalar", lambda e: e.activation(L.dts[:, 4:8], L.dts[:, 4:8], AF.Exp, scale=-1.0),
             reads=[("dsp",)], writes=[("dsp",)], tag=tg + f"sp2{c}")
        p.op("scalar", lambda e: e.activation(L.dts[:, 4:8], L.dts[:, 4:8], AF.Ln, bias=1.0),
             reads=[("dsp",)], writes=[("dsp",)], tag=tg + f"sp3{c}")
        p.op("vector", lambda e: e.tensor_scalar(L.drelu[:], L.dts[:, 0:4], 0.0, None, ALU.max),
             reads=[("dts",)], writes=[("drelu",)], tag=tg + f"sp4{c}")
        p.op("vector", lambda e: e.tensor_tensor(L.dts[:, 4:8], L.dts[:, 4:8], L.drelu[:], ALU.add),
             reads=[("dsp",), ("drelu",)], writes=[("dsp",)], tag=tg + f"sp5{c}")
        for sc in range(4):
            p.op("vector", lambda e, sc=sc: e.tensor_scalar(
                    L.btst[:, sc * 16:(sc + 1) * 16], L.bts[:, sc * 16:(sc + 1) * 16],
                    L.dts[:, 4 + sc: 5 + sc], None, ALU.mult),
                 reads=[("bts",), ("dsp",)], writes=[("btst",)], tag=tg + f"btd{c}_{sc}")
        p.op("vector", lambda e: e.tensor_copy(L.db16[:], L.dts[:, 4:8]),
             reads=[("dsp",)], writes=[("db16",)], tag=tg + f"db16{c}")
        def cumg(e):
            e.matmul(ps(6, 328, 4), kb.tri[:], L.db16[:], start=True, stop=False)
            r = None
            for j in range(1, 4):
                r = e.matmul(ps(6, 328 + j, 4 - j), kb.ones128[:], L.db16[:, 0:4 - j],
                             start=False, stop=(j == 3))
            return r
        p.op("tensor", cumg, reads=[("db16",), ("c", "tric"), ("c", "ones128")],
             writes=[pk(6)], tag=tg + f"cum{c}")
        p.op("scalar", lambda e: e.copy(L.cums[:], ps(6, 328, 4)),
             reads=[pk(6)], writes=[("cums",)], tag=tg + f"cume{c}")
        p.op("tensor", lambda e: e.transpose(ps(6, 332, 128)[0:4, :], L.cums[:], kb.identf[:]),
             reads=[("cums",), ("c", "fidentc")], writes=[pk(6)], tag=tg + f"cumt{c}")
        p.op("scalar", lambda e: e.copy(L.cumr4[:], ps(6, 332, 128)[0:4, :]),
             reads=[pk(6)], writes=[("cumr4",)], tag=tg + f"cumte{c}")
        p.dma("sync", L.cumflat[0:1, :].rearrange("a (s p) -> a s p", p=128), L.cumr4[:],
              "cfl", reads=[("cumr4",), ("cumflat",)], writes=[("cumflat",)], tag=tg + f"cfl{c}")
        if not (parts & 2):
            continue
        import os as _os5
        if int(_os5.environ.get("BCAST_DMA", "1")):
            p.dma("gpsimd", io["cum"][0:1, c * CH:(c + 1) * CH], L.cumflat[0:1, :],
                  "cms", reads=[("cumflat",)], writes=[("cum", c)], tag=tg + f"cms{c}")
            import dataclasses as _dc2
            _cmsrc = io["cum"][0:1, c * CH:(c + 1) * CH]
            _cmsrc = _dc2.replace(_cmsrc, ap=type(_cmsrc.ap)([[0, 128]] + list(_cmsrc.ap)[1:]))
            p.dma("gpsimd", L.ctbb[:], _cmsrc, "cml",
                  reads=[("ctbb",), ("cum", c)], writes=[("ctbb",)], tag=tg + f"cml{c}")
        p.op("scalar", lambda e: e.activation(L.ets[:], L.ctbb[0:16, :], AF.Exp,
                                              scale=kb.acols[0:16, 0:1]),
             reads=[("ctbb",), ("Acol",)], writes=[("ets",)], tag=tg + f"et{c}")
        p.op("vector", lambda e: e.tensor_copy(L.cbs[:], L.xdb[64:80, :]),
             reads=[("xdb",)], writes=[("cbs",)], tag=tg + f"cbe{c}")
        p.op("vector", lambda e: e.tensor_tensor(L.chat[:], L.cbs[:], L.ets[:], ALU.mult),
             reads=[("ets",), ("cbs",)], writes=[("chat",)], tag=tg + f"chat{c}")
        p.op("vector", lambda e: e.tensor_tensor(L.bqc[0:16, 0:1], kb.acols[0:16, 0:1],
                                                 L.ctbb[0:16, 511:512], ALU.mult),
             reads=[("ctbb",), ("Acol",)], writes=[("bqc",)], tag=tg + f"bq{c}")
        p.op("scalar", lambda e: e.activation(L.decs[:], L.ctbb[0:16, :], AF.Exp,
                                              scale=kb.acols[0:16, 1:2], bias=L.bqc[0:16, 0:1]),
             reads=[("ctbb",), ("bqc",), ("negA",)], writes=[("decs",)], tag=tg + f"dec{c}")

        def trdec(e):
            r = None
            for sc in range(4):
                r = e.transpose(psb16(6, 928 + sc * 16, 16), L.decs[:, sc * 128:(sc + 1) * 128], kb.ident[0:16, 0:16])
            return r
        p.op("tensor", trdec, reads=[("decs",), ("c", "identc")], writes=[pk(6)], tag=tg + f"trdc{c}")
        p.op("scalar", lambda e: e.copy(L.dect[:], psb16(6, 928, 64)),
             reads=[pk(6)], writes=[("dect",)], tag=tg + f"dce{c}")
        p.op("vector", lambda e: e.tensor_tensor(L.bhat[:], L.dect[:], L.btst[:], ALU.mult),
             reads=[("dect",), ("btst",)], writes=[("bhat",)], tag=tg + f"bhat{c}")
        p.op("scalar", lambda e: e.activation(L.bqc[0:16, 1:2], L.bqc[0:16, 0:1], AF.Exp),
             reads=[("bqc",)], writes=[("eq",)], tag=tg + f"eq{c}")
        p.op("scalar", lambda e: e.activation(L.hts[:], L.hts[:], AF.Copy,
                                              scale=L.bqc[0:16, 1:2]),
             reads=[("hts",), ("eq",)], writes=[("hts",)], tag=tg + f"hdec{c}")

        if not (parts & 4):
            continue
        import os as _os2
        scp = int(_os2.environ.get("SC_PARTS", "31"))
        for sc in range(4):
            if scp & 1:
                p.op("vector", lambda e, sc=sc: e.tensor_tensor(
                        L.ctm[:], L.ctbb[:, sc * 128:(sc + 1) * 128], kb.maskhuge[:], ALU.add),
                     reads=[("ctbb",), ("c", "fmaskhuge")], writes=[("ctm",)], tag=tg + f"ctm{c}_{sc}")
            if scp & 2:
                p.op("vector", lambda e, sc=sc: e.tensor_scalar(
                        L.biass[:], kb.nabc[:], L.cums[:, sc:sc + 1], None, ALU.mult),
                     reads=[("nabc",), ("cums",)], writes=[("biass",)], tag=tg + f"bias{c}_{sc}")
            def dtgen(e):
                r = None
                for n in range(16):
                    r = e.activation(L.dtbuf[:, n * 128:(n + 1) * 128], L.ctm[:], AF.Exp,
                                     scale=kb.abc[:, n:n + 1], bias=L.biass[:, n:n + 1])
                return r
            if scp & 4:
                p.op("scalar", dtgen, reads=[("ctm",), ("biass",), ("abc",)],
                     writes=[("dt",)], tag=tg + f"dt{c}_{sc}")

            def dbgen(e, sc=sc):
                r = None
                for n in range(16):
                    if kb.cfg.paranoid:
                        e.drain()
                    r = e.tensor_scalar(L.dtbuf[:, n * 128:(n + 1) * 128],
                                        L.dtbuf[:, n * 128:(n + 1) * 128],
                                        L.btst[:, sc * 16 + n: sc * 16 + n + 1], None, ALU.mult)
                return r
            if scp & 8:
                p.op("vector", dbgen, reads=[("dt",), ("btst",)], writes=[("dt",)],
                     tag=tg + f"db{c}_{sc}")

            def pacc(e, sc=sc):
                for n in range(16):
                    cslice = L.cbb[:, n * 512 + sc * 128: n * 512 + sc * 128 + 128]
                    if kb.cfg.paranoid:
                        e.drain()
                    if n == 0:
                        e.tensor_tensor(L.psb[:], L.dtbuf[:, 0:128], cslice, ALU.mult)
                    else:
                        e.tensor_tensor(L.ptmp[:], L.dtbuf[:, n * 128:(n + 1) * 128], cslice, ALU.mult)
                        if kb.cfg.paranoid:
                            e.drain()
                        e.tensor_tensor(L.psb[:], L.psb[:], L.ptmp[:], ALU.add)
                if kb.cfg.paranoid:
                    e.drain()
                return e.tensor_copy(L.pb4[:, sc * 128:(sc + 1) * 128], L.psb[:])
            if scp & 16:
                p.op("vector", pacc, reads=[("dt",), ("cbb",)], writes=[("P",), ("pb4",)],
                     tag=tg + f"pacc{c}_{sc}")
            for g in range(2 if (parts & 8) else 0):
                def trx(e, sc=sc, g=g):
                    r = None
                    for i in range(8):
                        dt = g * 8 + i
                        tb = 2 + i // 4
                        r = e.transpose(psb16(tb, (i % 4) * 128, 128),
                                        L.xss[:, dt * 512 + sc * 128: dt * 512 + sc * 128 + 128],
                                        kb.ident[:])
                    return r
                p.op("tensor", trx, reads=[("xss",), ("c", "identc")],
                     writes=[pk(2), pk(3)], tag=tg + f"xst{c}_{sc}_{g}")

                def cpx(e, sc=sc, g=g):
                    r = None
                    for i in range(8):
                        dt = g * 8 + i
                        tb = 2 + i // 4
                        r = e.copy(L.xstt[:, dt * 512 + sc * 128: dt * 512 + sc * 128 + 128],
                                   psb16(tb, (i % 4) * 128, 128))
                    return r
                p.op("scalar", cpx, reads=[pk(2), pk(3)], writes=[("xstt",)],
                     tag=tg + f"xse{c}_{sc}_{g}")

        for dt in range(16 if (parts & 16) else 0):
            by = 6 + dt % 2

            def mmy(e, dt=dt, by=by):
                r = None
                for sc in range(4):
                    e.matmul(ps(by, sc * 128, 128), L.xstt[:, dt * 512 + sc * 128: dt * 512 + sc * 128 + 128],
                             L.pb4[:, sc * 128:(sc + 1) * 128], start=True, stop=False)
                    r = e.matmul(ps(by, sc * 128, 128), L.htb[0:16, dt * 128:(dt + 1) * 128],
                                 L.chat[0:16, sc * 128:(sc + 1) * 128], start=False, stop=True)
                return r
            p.op("tensor", mmy, reads=[("xstt",), ("pb4",), ("htb",), ("chat",)],
                 writes=[pk(by)], tag=tg + f"y{c}_{dt}")
            p.op("vector", lambda e, dt=dt: e.tensor_scalar(
                    L.yw[:], L.xss[:, dt * 512:(dt + 1) * 512], kb.dcol[:, dt:dt + 1], None, ALU.mult),
                 reads=[("xss",), ("c", "fdcol")], writes=[("yu",)], tag=tg + f"yu{c}_{dt}")
            p.op("vector", lambda e, dt=dt, by=by: e.tensor_tensor(
                    L.yw[:], ps(by, 0, 512), L.yw[:], ALU.add),
                 reads=[pk(by), ("yu",)], writes=[("yu",)], tag=tg + f"yw{c}_{dt}")
            p.op("vector", lambda e, dt=dt: e.tensor_tensor(
                    L.ys[:, dt * 512:(dt + 1) * 512], L.yw[:], L.szs[:, dt * 512:(dt + 1) * 512], ALU.mult),
                 reads=[("yu",), ("szs",)], writes=[("ys", dt)], tag=tg + f"yo{c}_{dt}")

        for dt in range(16 if (parts & 32) else 0):
            def mmh(e, dt=dt):
                r = None
                for sc in range(4):
                    r = e.matmul(ps(4, 256, 128)[0:16, :], L.bhat[:, sc * 16:(sc + 1) * 16],
                                 L.xstt[:, dt * 512 + sc * 128: dt * 512 + sc * 128 + 128],
                                 start=(sc == 0), stop=(sc == 3))
                return r
            p.op("tensor", mmh, reads=[("bhat",), ("xstt",)], writes=[pk(4)], tag=tg + f"hmm{c}_{dt}")
            p.op("vector", lambda e, dt=dt: e.tensor_tensor(
                    L.hts[0:16, dt * 128:(dt + 1) * 128], ps(4, 256, 128)[0:16, :],
                    L.hts[0:16, dt * 128:(dt + 1) * 128], ALU.add),
                 reads=[pk(4), ("hts",)], writes=[("hts",)], tag=tg + f"hup{c}_{dt}")
        if parts & 32:
            p.op("vector", lambda e: e.tensor_copy(L.htb[:], L.hts[:]),
                 reads=[("hts",)], writes=[("htb",)], tag=tg + f"htb{c}")

        import os as _os3
        wop = int(_os3.environ.get("WOUT_PARTS", "7"))
        for m in range(8 if (parts & 64) else 0):
            bw = m % 2

            def mmo(e, m=m, bw=bw):
                r = None
                for k in range(16):
                    r = e.matmul(ps(bw, 0, 512), L.woutb[:, k * DM + m * 128: k * DM + m * 128 + 128],
                                 L.ys[:, k * 512:(k + 1) * 512], start=(k == 0), stop=(k == 15))
                return r
            if wop & 1:
                p.op("tensor", mmo, reads=[("ys", k) for k in range(16)] + [("woutb",)],
                     writes=[pk(bw)], tag=tg + f"woc{c}_{m}")
            if wop & 2:
                p.op("vector", lambda e, m=m, bw=bw, xs=xs: e.tensor_tensor(
                        kb.vout[m % 2][:], ps(bw, 0, 512), xs[:, m * 512:(m + 1) * 512], ALU.add),
                     reads=[pk(bw), ("xst", c % 2)], writes=[("voutbuf", m % 2)], tag=tg + f"x2o{c}_{m}")
            if wop & 4:
                p.dma("gpsimd", xout[:, c * CH:(c + 1) * CH].rearrange("(a p) n -> p a n", p=128)[:, m, :],
                      kb.vout[m % 2][:], f"vst{m % 2}", reads=[("voutbuf", m % 2)],
                      writes=[("x2o", c, m)], tag=tg + f"stx{c}_{m}")
    kb.close_scope()



def build(cfg, stop=None):
    kb = KB(cfg)
    p, io = kb.prog, kb.io
    step = [0]

    def go():
        step[0] += 1
        return stop is None or step[0] <= stop
    for b in range(cfg.NB):
        xb = io["x0"][b]
        ob = io["out"][b]
        for v in ("vA", "vB", "vC"):
            p.op("vector", lambda e: e.memset(kb.zc[:, 0:8], 0.0), reads=[], writes=[("z",)],
                 tag=f"b{b}z{v}")
            p.dma("gpsimd", io[v][:, 0:1].rearrange("(a p) n -> p a n", p=128),
                  kb.zc[:, 0:8].rearrange("p (a n) -> p a n", n=1), "vst0", reads=[("z",)], writes=[(v, -1)],
                  allow_slow_non_contiguous=True, tag=f"b{b}zg{v}")
        p.barrier()
        for half, xi, xo in ((0, xb, io["x1"]), (1, io["x2"], ob)):
            t = f"b{b}h{half}"
            if go():
                pass_attn_qkv(kb, t + "A0", 0, xi, False, "vA", first_ln=True)
                p.barrier()
            if go():
                pass_attn_core(kb, t + "B0", 0, "vA", True, io["vB"])
                p.barrier()
            if go():
                pass_mlp1(kb, t + "C0", 0, "vB")
                p.barrier()
            if go():
                pass_mlp2(kb, t + "D0", 0, "vB", io["vC"], False, None, False)
                p.barrier()
            if go():
                pass_attn_qkv(kb, t + "A1", 1, io["vC"], True, "vC", first_ln=False)
                p.barrier()
            if go():
                pass_attn_core(kb, t + "B1", 1, "vC", True, io["vA"])
                p.barrier()
            if go():
                pass_mlp1(kb, t + "C1", 1, "vA")
                p.barrier()
            if go():
                pass_mlp2(kb, t + "D1", 1, "vA", xo, True, xi, False)
                p.barrier()
            if half == 0:
                if go():
                    pass_mamba1(kb, t + "M1", io["x1"])
                    p.barrier()
                if go():
                    import os as _os
                    pass_mamba2(kb, t + "M2", io["x1"], io["x2"],
                                parts=int(_os.environ.get("MB2_PARTS", "127")))
                    p.barrier()
    stats = p.emit()
    return kb, stats



def kernel(**inputs):
    cfg = Cfg(T=4096, NB=2)
    consts = host_consts(cfg)
    weights = host_weights(inputs)
    x = np.ascontiguousarray(np.transpose(np.asarray(inputs["L0_attn_x"]), (0, 2, 1)))
    shards = [np.ascontiguousarray(x[2 * i:2 * i + 2]) for i in range(8)]
    kb, stats = build(cfg)
    shared = {**consts, **weights}
    in_maps = [dict(shared, x0=s) for s in shards]
    res = run_bass_kernel_spmd(kb.nc, in_maps, core_ids=list(range(8)))
    outs = [np.asarray(r["out"]) for r in res.results]
    full = np.concatenate(outs, axis=0)
    return np.ascontiguousarray(np.transpose(full, (0, 2, 1)))
